# version 04
# baseline (speedup 1.0000x reference)
import contextlib
import os
import numpy as np
import ml_dtypes
import concourse.bass as bass
import concourse.mybir as mybir
from concourse.bass_utils import run_bass_kernel_spmd

F32 = mybir.dt.float32
BF16 = mybir.dt.bfloat16
I32 = mybir.dt.int32
AF = mybir.ActivationFunctionType
ALU = mybir.AluOpType
AX = mybir.AxisListType.X

D = 1024
NCH = D // 128
H_FFN = 4096
LN_EPS = 1e-5
RMS_EPS = float(np.finfo(np.float32).eps)

ENGINES = ("sp", "pe", "dve", "act", "pool")
_SECTION = {"sp": "sync", "pe": "tensor", "dve": "vector", "act": "scalar", "pool": "gpsimd"}


class Step:
    __slots__ = ("eng", "sem", "fn", "deps", "inc", "cum")

    def __init__(self, eng, sem, fn, deps, inc, cum):
        self.eng, self.sem, self.fn = eng, sem, fn
        self.deps, self.inc, self.cum = deps, inc, cum


class Prog:

    def __init__(self, nc, stack):
        self.nc = nc
        self.stack = stack
        self.sems = {}
        self.cum = {}
        for e in ENGINES:
            self.sems[e] = stack.enter_context(nc.semaphore("tk_" + e))
            self.cum[e] = 0
        for e in ("sp", "pool", "act", "dve"):
            self.sems[e + "~d"] = stack.enter_context(nc.semaphore("td_" + e))
            self.cum[e + "~d"] = 0
        self.const = {}

    def block(self, name=""):
        return Blk(self, name)


class Blk:

    def __init__(self, p, name):
        self.p = p
        self.name = name
        self.steps = []

    def step(self, eng, fn, deps=(), inc=1, _sem=None):
        sem = _sem or eng
        self.p.cum[sem] += inc
        s = Step(eng, sem, fn, tuple(d for d in deps if d is not None), inc,
                 self.p.cum[sem])
        self.steps.append(s)
        return s

    def dma(self, eng, fn, deps=(), n=1):
        return self.step(eng, fn, deps, inc=16 * n, _sem=eng + "~d")

    def run(self):
        p = self.p
        with p.nc.Block(self.name or None) as blk:
            for ename in ENGINES:
                steps = [s for s in self.steps if s.eng == ename]
                if not steps:
                    continue
                section = getattr(blk, _SECTION[ename])

                def body(eng, steps=steps, ename=ename):
                    for s in steps:
                        waits = {}
                        for d in s.deps:
                            if d.eng == ename and d.sem == ename:
                                waits[d.sem] = max(waits.get(d.sem, 0), d.cum)
                            else:
                                waits[d.sem] = max(waits.get(d.sem, 0), d.cum)
                        for sm, v in waits.items():
                            eng.wait_ge(p.sems[sm], v)
                        out = s.fn(eng)
                        sem = p.sems[s.sem]
                        if isinstance(out, list):
                            tot = 0
                            for inst, n in out:
                                if n:
                                    inst.then_inc(sem, n)
                                tot += n
                            assert tot == s.inc, f"step inc mismatch {tot} != {s.inc}"
                        else:
                            out.then_inc(sem, s.inc)

                section(body)
        self.steps = None


def emit_constants(p):
    nc = p.nc
    C = p.const
    C["ident_b"] = p.stack.enter_context(nc.sbuf_tensor("ident_b", [128, 128], BF16))
    C["ident_f"] = p.stack.enter_context(nc.sbuf_tensor("ident_f", [128, 128], F32))
    C["ones_col_b"] = p.stack.enter_context(nc.sbuf_tensor("ones_col_b", [128, 1], BF16))
    C["ones_col_f"] = p.stack.enter_context(nc.sbuf_tensor("ones_col_f", [128, 1], F32))
    C["ones_row_b"] = p.stack.enter_context(nc.sbuf_tensor("ones_row_b", [1, 128], BF16))
    C["ones_row_f"] = p.stack.enter_context(nc.sbuf_tensor("ones_row_f", [1, 128], F32))
    C["eps_ln"] = p.stack.enter_context(nc.sbuf_tensor("eps_ln", [128, 1], F32))
    C["eps_rms"] = p.stack.enter_context(nc.sbuf_tensor("eps_rms", [128, 1], F32))
    with nc.sbuf_tensor("ii1", [128, 128], I32) as ii1, \
            nc.sbuf_tensor("ii2", [128, 128], I32) as ii2:
        b = p.block("consts")
        s0 = b.step("pool", lambda g: [
            (g.iota(ii1[:], [[1, 128]], channel_multiplier=0), 0),
            (g.iota(ii2[:], [[0, 128]], channel_multiplier=1), 0),
            (g.memset(C["ones_col_b"][:], 1.0), 0),
            (g.memset(C["ones_col_f"][:], 1.0), 0),
            (g.memset(C["ones_row_b"][:], 1.0), 0),
            (g.memset(C["ones_row_f"][:], 1.0), 0),
            (g.memset(C["eps_ln"][:], LN_EPS), 0),
            (g.memset(C["eps_rms"][:], RMS_EPS), 1),
        ], inc=1)
        s1 = b.step("dve", lambda v: v.tensor_tensor(ii1[:], ii1[:], ii2[:],
                                                     ALU.is_equal), deps=[s0])
        s2 = b.step("dve", lambda v: v.tensor_copy(C["ident_b"][:], ii1[:]), deps=[s1])
        b.step("dve", lambda v: v.tensor_copy(C["ident_f"][:], ii1[:]), deps=[s2])
        b.run()


def load_bias_cols(b, dst, src_ap, n, deps=()):
    def fn(s):
        out = []
        for c in range(n):
            out.append((s.dma_start(dst[:, c:c + 1],
                                    src_ap[c * 128:(c + 1) * 128].unsqueeze(1),
                                    allow_slow_non_contiguous=True), 16))
        return out
    return b.dma("sp", fn, deps=deps, n=n)


def emit_phase_t(p, x_ap, xfm_ap, nb, seq):
    nc = p.nc
    ident_b = p.const["ident_b"]
    ngrp = nb * seq // 512
    with contextlib.ExitStack() as ctx:
        xtok = [ctx.enter_context(nc.sbuf_tensor(f"pt_xtok{i}", [128, 1024], BF16))
                for i in range(8)]
        stage = [ctx.enter_context(nc.sbuf_tensor(f"pt_stage{i}", [128, NCH * 512], BF16))
                 for i in range(2)]
        ps = [ctx.enter_context(nc.psum_tensor(f"pt_ps{i}", [128, 128], BF16))
              for i in range(4)]
        b = p.block("phase_t")
        tload = {}
        ttrans = {}
        tevac = {}
        tstore = {}
        for gi in range(ngrp):
            for tb in range(4):
                i = gi * 4 + tb
                tok0 = gi * 512 + tb * 128
                bi, s0 = tok0 // seq, tok0 % seq
                dep = ttrans.get((i - 8, NCH - 1))
                tload[i] = b.dma("pool", lambda g, i=i, bi=bi, s0=s0:
                                 g.dma_start(xtok[i % 8][:], x_ap[bi, s0:s0 + 128, :]),
                                 deps=[dep])
                for c in range(NCH):
                    j = i * 8 + c
                    deps = [tload[i], tevac.get(divmod(j - 4, 8))]
                    ttrans[(i, c)] = b.step(
                        "pe", lambda t, i=i, c=c, j=j:
                        t.transpose(ps[j % 4][:], xtok[i % 8][:, c * 128:(c + 1) * 128],
                                    ident_b[:]),
                        deps=deps)
                    gdep = tstore.get(gi - 2) if (tb == 0 and c == 0) else None
                    tevac[(i, c)] = b.step(
                        "dve", lambda v, gi=gi, tb=tb, c=c, j=j:
                        v.tensor_copy(stage[gi % 2][:, c * 512 + tb * 128:
                                                    c * 512 + tb * 128 + 128], ps[j % 4][:]),
                        deps=[ttrans[(i, c)], gdep])
            def store_fn(s, gi=gi):
                return [(s.dma_start(xfm_ap[c, :, gi * 512:(gi + 1) * 512],
                                     stage[gi % 2][:, c * 512:(c + 1) * 512]), 16)
                        for c in range(NCH)]
            tstore[gi] = b.dma("sp", store_fn,
                               deps=[tevac[(gi * 4 + 3, c)] for c in range(NCH)], n=NCH)
        last = tstore[ngrp - 1]
        b.step("sp", lambda s: s.nop(), deps=[last])
        b.run()


class War:

    def __init__(self):
        self.readers = {}

    def r(self, key, step):
        self.readers.setdefault(key, []).append(step)

    def w(self, key):
        return self.readers.pop(key, [])


def emit_l0_attn(p, xfm_in, xfm_out, W, nb, seq, convert_jobs=None):
    nc = p.nc
    ntc = seq // 512
    with contextlib.ExitStack() as ctx:
        sb = lambda n, s, d: ctx.enter_context(nc.sbuf_tensor(n, s, d))
        cvbufs = [ctx.enter_context(nc.sbuf_tensor(f"cv_b{i}", [128, 4096], BF16))
                  for i in range(4)] if convert_jobs else None
        psum = lambda n, s, d=F32: ctx.enter_context(nc.psum_tensor(n, s, d))
        wsb = {k: sb(f"l0a_{k}", [128, NCH * 1024], BF16) for k in ["wq", "wk", "wv", "wo"]}
        bsb = {k: sb(f"l0a_{k}", [128, NCH], F32) for k in ["bq", "bk", "bv", "bo"]}
        xbuf = [sb(f"l0a_x{i}", [128, NCH * 512], BF16) for i in range(2)]
        kvacc = [sb(f"l0a_kv{i}", [128, NCH], F32) for i in range(nb)]
        ksacc = [sb(f"l0a_ks{i}", [128, NCH], F32) for i in range(nb)]
        red = sb("l0a_red", [128, 2], F32)
        pre = [sb(f"l0a_pre{i}", [128, 512], F32) for i in range(2)]
        mn = [sb(f"l0a_mn{i}", [128, 512], F32) for i in range(2)]
        ex = [sb(f"l0a_ex{i}", [128, 512], F32) for i in range(2)]
        kf = [sb(f"l0a_kf{i}", [128, 512], BF16) for i in range(2)]
        vt = [sb(f"l0a_vt{i}", [128, 512], BF16) for i in range(2)]
        prod = [sb(f"l0a_prod{i}", [128, 512], BF16) for i in range(2)]
        qf = [sb(f"l0a_qf{i}", [128, 512], F32) for i in range(2)]
        t1 = [sb(f"l0a_t1{i}", [128, 512], F32) for i in range(2)]
        abuf = sb("l0a_abuf", [128, NCH * 512], BF16)
        gt = [sb(f"l0a_gt{i}", [128, 512], BF16) for i in range(2)]
        ln = mk_ln_tiles(ctx, nc, "l0a")
        stage = [sb(f"l0a_stg{i}", [128, NCH * 512], BF16) for i in range(2)]
        pMM = [psum(f"l0a_pmm{i}", [128, 512]) for i in range(4)]

        b = p.block("l0a_w")
        for k in ["wq", "wk", "wv", "wo"]:
            b.dma("pool", lambda g, k=k: [
                (g.dma_start(wsb[k][:, dc * 1024:(dc + 1) * 1024],
                             W[k.capitalize()][dc * 128:(dc + 1) * 128, :]), 16)
                for dc in range(NCH)], n=NCH)
        for k in ["bq", "bk", "bv", "bo"]:
            load_bias_cols(b, bsb[k], W[k], NCH)
        b.run()

        b = p.block("l0a_main")
        war = War()
        cv_last = None
        if convert_jobs:
            cv_last = emit_convert_steps(p, b, War(), cvbufs, convert_jobs)
        st_steps = {}
        it = 0

        def load_x(bi, tc, it):
            slot = it % 2
            step = b.dma("sp", lambda s, bi=bi, tc=tc, slot=slot: [
                (s.dma_start(xbuf[slot][:, dc * 512:(dc + 1) * 512],
                             xfm_in[dc, :, bi * seq + tc * 512: bi * seq + tc * 512 + 512]), 16)
                for dc in range(NCH)], deps=war.w(("x", slot)), n=NCH)
            return step, slot

        def proj_chain(wkey, c, slot, bank, deps):
            return b.step("pe", lambda t, wkey=wkey, c=c, slot=slot, bank=bank: [
                (t.matmul(bank[:], wsb[wkey][:, dc * 1024 + c * 128: dc * 1024 + c * 128 + 128],
                          xbuf[slot][:, dc * 512:(dc + 1) * 512],
                          start=(dc == 0), stop=(dc == NCH - 1)),
                 1 if dc == NCH - 1 else 0)
                for dc in range(NCH)], deps=deps)

        def fmap(bank, bias_key, c, r, out_tile, mm):
            prek = b.step("dve", lambda v, bank=bank, c=c, r=r: v.tensor_scalar(
                pre[r][:], bank[:], bsb[bias_key][:, c:c + 1], None, ALU.add),
                deps=[mm] + war.w(("pre", r)))
            mnk = b.step("dve", lambda v, r=r: v.tensor_scalar(
                mn[r][:], pre[r][:], 0.0, None, ALU.min),
                deps=[prek] + war.w(("mn", r)))
            exk = b.step("act", lambda a, r=r: a.activation(ex[r][:], mn[r][:], AF.Exp),
                         deps=[mnk] + war.w(("ex", r)))
            war.r(("mn", r), exk)
            fin = b.step("dve", lambda v, r=r, out_tile=out_tile: v.scalar_tensor_tensor(
                out_tile[r][:], pre[r][:], 0.0, ex[r][:], ALU.max, ALU.add),
                deps=[exk] + war.w((out_tile is kf and "kf" or "qf", r)))
            war.r(("pre", r), fin)
            war.r(("ex", r), fin)
            return prek, fin

        for bi in range(nb):
            z0 = b.step("pool", lambda g, bi=bi: [
                (g.memset(kvacc[bi][:], 0.0), 0),
                (g.memset(ksacc[bi][:], 0.0), 1)])
            war.r(("kvz", bi), z0)
            for tc in range(ntc):
                ld, slot = load_x(bi, tc, it)
                it += 1
                for c in range(NCH):
                    j = tc * NCH + c
                    r = j % 2
                    bk_, bv_ = pMM[r], pMM[2 + r]
                    mmk = proj_chain("wk", c, slot, bk_, [ld] + war.w(("bank", r)))
                    war.r(("x", slot), mmk)
                    prek, kfs = fmap(bk_, "bk", c, r, kf, mmk)
                    war.r(("bank", r), prek)
                    mmv = proj_chain("wv", c, slot, bv_, [ld] + war.w(("bank", 2 + r)))
                    war.r(("x", slot), mmv)
                    vts = b.step("act", lambda a, bv_=bv_, c=c, r=r: a.activation(
                        vt[r][:], bv_[:], AF.Identity, bias=bsb["bv"][:, c:c + 1]),
                        deps=[mmv] + war.w(("vt", r)))
                    war.r(("bank", 2 + r), vts)
                    prods = b.step("dve", lambda v, r=r: v.tensor_tensor(
                        prod[r][:], kf[r][:], vt[r][:], ALU.mult),
                        deps=[vts, kfs] + war.w(("prod", r)))
                    war.r(("vt", r), prods)
                    red_s = b.step("dve", lambda g, r=r: [
                        (g.reduce_sum(red[:, 0:1], prod[r][:], axis=AX), 0),
                        (g.reduce_sum(red[:, 1:2], kf[r][:], axis=AX), 1)],
                        deps=[prods] + war.w(("red",)))
                    acc = b.step("dve", lambda g, c=c, bi=bi: [
                        (g.tensor_tensor(kvacc[bi][:, c:c + 1], kvacc[bi][:, c:c + 1],
                                         red[:, 0:1], ALU.add), 0),
                        (g.tensor_tensor(ksacc[bi][:, c:c + 1], ksacc[bi][:, c:c + 1],
                                         red[:, 1:2], ALU.add), 1)],
                        deps=[red_s] + war.w(("kvz", bi)))
                    war.r(("red",), acc)
                    war.r(("prod", r), red_s)
                    war.r(("kf", r), red_s)
            war.readers[("kvdone", bi)] = [b.steps[-1]]

        ln_war = {}
        for bi in range(nb):
            kvd = war.readers[("kvdone", bi)][0]
            for tc in range(ntc):
                ld, slot = load_x(bi, tc, it)
                it += 1
                for c in range(NCH):
                    j = tc * NCH + c
                    r = j % 2
                    bq_ = pMM[r]
                    mmq = proj_chain("wq", c, slot, bq_, [ld, kvd] + war.w(("bank", r)))
                    war.r(("x", slot), mmq)
                    preq, qfs = fmap(bq_, "bq", c, r, qf, mmq)
                    war.r(("bank", r), preq)
                    ts1 = b.step("dve", lambda v, r=r, c=c, bi=bi: [
                        (v.tensor_scalar(t1[r][:], qf[r][:], kvacc[bi][:, c:c + 1], None, ALU.mult), 0),
                        (v.tensor_scalar(mn[r][:], qf[r][:], ksacc[bi][:, c:c + 1], 1e-6,
                                         ALU.mult, ALU.add), 1)],
                        deps=[qfs] + war.w(("t1", r)) + war.w(("mn", r)))
                    war.r(("qf", r), ts1)
                    lnq = b.step("act", lambda a, r=r: a.activation(ex[r][:], mn[r][:], AF.Ln),
                                 deps=[ts1] + war.w(("ex", r)))
                    war.r(("mn", r), lnq)
                    rec = b.step("act", lambda a, r=r: a.activation(ex[r][:], ex[r][:], AF.Exp,
                                                                   scale=-1.0), deps=[lnq])
                    aw = b.step("dve", lambda v, r=r, c=c: v.tensor_tensor(
                        abuf[:, c * 512:(c + 1) * 512], t1[r][:], ex[r][:], ALU.mult),
                        deps=[rec] + war.w(("ab", c)))
                    war.r(("t1", r), aw)
                    war.r(("ex", r), aw)
                adone = aw
                rcs_last = None
                for co in range(NCH):
                    j = tc * NCH + co
                    r = j % 2
                    bo_ = pMM[2 + r]
                    mmo = b.step("pe", lambda t, co=co, bo_=bo_: [
                        (t.matmul(bo_[:], wsb["wo"][:, cc * 1024 + co * 128: cc * 1024 + co * 128 + 128],
                                  abuf[:, cc * 512:(cc + 1) * 512],
                                  start=(cc == 0), stop=(cc == NCH - 1)),
                         1 if cc == NCH - 1 else 0)
                        for cc in range(NCH)],
                        deps=[adone] + war.w(("bank", 2 + r)))
                    for cc in range(NCH):
                        war.r(("ab", cc), mmo)
                    gls = b.step("act", lambda a, bo_=bo_, co=co, r=r: a.activation(
                        gt[r][:], bo_[:], AF.Gelu_apprx_tanh, bias=bsb["bo"][:, co:co + 1]),
                        deps=[mmo] + war.w(("gt", r)))
                    war.r(("bank", 2 + r), gls)
                    rcs_a = b.step("dve", lambda v, r=r, co=co, slot=slot: v.tensor_tensor(
                        ln["rbuf"][:, co * 512:(co + 1) * 512], gt[r][:],
                        xbuf[slot][:, co * 512:(co + 1) * 512], ALU.add),
                        deps=[gls] + war.w(("rb", co)))
                    rcs_last = b.step("dve", lambda v, co=co: v.tensor_copy(
                        ln["rbb"][:, co * 512:(co + 1) * 512],
                        ln["rbuf"][:, co * 512:(co + 1) * 512]), deps=[rcs_a])
                    war.r(("gt", r), rcs_last)
                    war.r(("x", slot), rcs_last)
                out_step = emit_ln(p, b, ln, war, rcs_last, p.const["eps_ln"],
                                   stage[tc % 2], war.w(("stg", tc % 2)))
                st = b.dma("sp", lambda s, bi=bi, tc=tc: [
                    (s.dma_start(xfm_out[c, :, bi * seq + tc * 512: bi * seq + tc * 512 + 512],
                                 stage[tc % 2][:, c * 512:(c + 1) * 512]), 16)
                    for c in range(NCH)], deps=[out_step], n=NCH)
                war.r(("stg", tc % 2), st)
                st_steps[(bi, tc)] = st
        b.step("sp", lambda s: s.nop(), deps=[st_steps[(nb - 1, ntc - 1)], cv_last])
        b.run()


def mk_ln_tiles(ctx, nc, tag, ext_psum=False, rbb=True, sq_dtype=BF16):
    sb = lambda n, s, d: ctx.enter_context(nc.sbuf_tensor(n, s, d))
    ps = {} if ext_psum else dict(
        pS1=ctx.enter_context(nc.psum_tensor(f"{tag}_pS1", [1, 512], F32)),
        pS2=ctx.enter_context(nc.psum_tensor(f"{tag}_pS2", [1, 512], F32)),
        pB1=ctx.enter_context(nc.psum_tensor(f"{tag}_pB1", [128, 512], F32)),
        pB2=ctx.enter_context(nc.psum_tensor(f"{tag}_pB2", [128, 512], F32)),
    )
    rb = dict(rbb=sb(f"{tag}_rbb", [128, NCH * 512], BF16)) if rbb else {}
    return dict(
        **ps, **rb,
        rbuf=sb(f"{tag}_rbuf", [128, NCH * 512], F32),
        sqb=[sb(f"{tag}_sqb{i}", [128, 512], sq_dtype) for i in range(2)],
        s1row=sb(f"{tag}_s1row", [1, 512], F32),
        s2row=sb(f"{tag}_s2row", [1, 512], F32),
        mb=sb(f"{tag}_mb", [128, 512], F32),
        s2m=sb(f"{tag}_s2m", [128, 512], F32),
        rstd=sb(f"{tag}_rstd", [128, 512], F32),
        scr=sb(f"{tag}_scr", [128, 512], F32),
    )


def emit_ln(p, b, ln, war, rdep, eps, stage, stage_war, use_rbb=True,
            out_inplace_f32=False):
    C = p.const
    pS1, pS2, pB1, pB2 = ln["pS1"], ln["pS2"], ln["pB1"], ln["pB2"]
    _kd = ln.get("_k", {})
    kmap = {id(pS1): _kd.get("pS1"), id(pS2): _kd.get("pS2"),
            id(pB1): _kd.get("pB1"), id(pB2): _kd.get("pB2")}
    k = lambda t: kmap.get(id(t)) or ("ps", id(t))
    ones = C["ones_col_b"] if use_rbb else C["ones_col_f"]
    ones_sq = C["ones_col_b"] if ln["sqb"][0].dtype == BF16 else ones
    srcof = (lambda c: ln["rbb"][:, c * 512:(c + 1) * 512]) if use_rbb else \
            (lambda c: ln["rbuf"][:, c * 512:(c + 1) * 512])
    mms = None
    sqs = {}
    for c in range(NCH):
        sqs[c] = b.step("act", lambda a, c=c: a.activation(
            ln["sqb"][c % 2][:], srcof(c), AF.Square),
            deps=[rdep] + war.w(("lnsq", c % 2)))
        mms = b.step("pe", lambda t, c=c: [
            (t.matmul(pS1[:], ones[:], srcof(c),
                      start=(c == 0), stop=(c == NCH - 1), skip_group_check=True), 0),
            (t.matmul(pS2[:], ones_sq[:], ln["sqb"][c % 2][:],
                      start=(c == 0), stop=(c == NCH - 1), skip_group_check=True), 1)],
            deps=[sqs[c], rdep] + (war.w(k(pS1)) + war.w(k(pS2)) if c == 0 else []))
        war.r(("lnsq", c % 2), mms)
        war.r(("rb", c), sqs[c])
        war.r(("rb", c), mms)
    rows = b.step("dve", lambda v: [
        (v.tensor_copy(ln["s1row"][:], pS1[:]), 0),
        (v.tensor_copy(ln["s2row"][:], pS2[:]), 1)],
        deps=[mms] + war.w(("lnrow",)))
    war.r(k(pS1), rows)
    war.r(k(pS2), rows)
    bc = b.step("pe", lambda t: [
        (t.matmul(pB1[:], C["ones_row_f"][:], ln["s1row"][:], start=True, stop=True), 0),
        (t.matmul(pB2[:], C["ones_row_f"][:], ln["s2row"][:], start=True, stop=True), 1)],
        deps=[rows] + war.w(k(pB1)) + war.w(k(pB2)))
    war.r(("lnrow",), bc)
    stats = b.step("act", lambda a: [
        (a.activation(ln["mb"][:], pB1[:], AF.Copy, scale=1.0 / D), 0),
        (a.activation(ln["s2m"][:], pB2[:], AF.Copy, scale=1.0 / D), 1)],
        deps=[bc] + war.w(("lnstat",)))
    war.r(k(pB1), stats)
    war.r(k(pB2), stats)
    msq = b.step("act", lambda a: a.activation(ln["scr"][:], ln["mb"][:], AF.Square),
                 deps=[stats])
    var = b.step("dve", lambda v: v.tensor_tensor(ln["s2m"][:], ln["s2m"][:], ln["scr"][:],
                                                  ALU.subtract), deps=[msq])
    rs1 = b.step("act", lambda a, eps=eps: a.activation(ln["rstd"][:], ln["s2m"][:],
                                                        AF.Ln, bias=eps[:]), deps=[var])
    rst = b.step("act", lambda a: a.activation(ln["rstd"][:], ln["rstd"][:], AF.Exp,
                                               scale=-0.5), deps=[rs1])
    out = None
    osub = None
    for c in range(NCH):
        osub = b.step("dve", lambda v, c=c: v.tensor_tensor(
            ln["scr"][:], ln["rbuf"][:, c * 512:(c + 1) * 512], ln["mb"][:], ALU.subtract),
            deps=[rst, osub] + (stage_war if c == 0 else []))
        if out_inplace_f32:
            out = b.step("dve", lambda v, c=c: v.tensor_tensor(
                ln["rbuf"][:, c * 512:(c + 1) * 512], ln["scr"][:], ln["rstd"][:],
                ALU.mult), deps=[osub])
        else:
            out = b.step("dve", lambda v, c=c: v.tensor_tensor(
                stage[:, c * 512:(c + 1) * 512], ln["scr"][:], ln["rstd"][:], ALU.mult),
                deps=[osub])
        war.r(("rb", c), osub)
        war.r(("rb", c), out)
    war.r(("lnstat",), out)
    return out


def emit_convert_steps(p, b, war, bufs, jobs):
    for src, dst, mode, gain in jobs:
        R, Cc = src.shape
        nband = R // 128
        for hc in range(nband):
            slot = hc % 4
            ld = b.dma("pool", lambda g, src=src, hc=hc, slot=slot, Cc=Cc:
                       g.dma_start(bufs[slot][:, 0:Cc], src[hc * 128:(hc + 1) * 128, :]),
                       deps=war.w(("b", slot)))
            dep = ld
            if gain is not None:
                dep = b.step("dve", lambda v, slot=slot, gain=gain, hc=hc, Cc=Cc:
                             v.tensor_scalar(bufs[slot][:, 0:Cc], bufs[slot][:, 0:Cc],
                                             gain[:, hc:hc + 1], None, ALU.mult),
                             deps=[ld])
            if mode == "copy":
                st = b.dma("sp", lambda s, dst=dst, hc=hc, slot=slot, Cc=Cc:
                           s.dma_start(dst[hc * 128:(hc + 1) * 128, :], bufs[slot][:, 0:Cc]),
                           deps=[dep])
                war.r(("b", slot), st)
            else:
                def fn(s, dst=dst, hc=hc, slot=slot, Cc=Cc):
                    return [(s.dma_start(dst[cb, :, hc * 128:(hc + 1) * 128],
                                         bufs[slot][:, cb * 128:(cb + 1) * 128]), 16)
                            for cb in range(Cc // 128)]
                st = b.dma("sp", fn, deps=[dep], n=Cc // 128)
                war.r(("b", slot), st)
    return st


def emit_convert(p, jobs):
    nc = p.nc
    with contextlib.ExitStack() as ctx:
        bufs = [ctx.enter_context(nc.sbuf_tensor(f"cv_b{i}", [128, 4096], BF16))
                for i in range(4)]
        b = p.block("convert")
        st = emit_convert_steps(p, b, War(), bufs, jobs)
        b.step("sp", lambda s: s.nop(), deps=[st])
        b.run()


def emit_router_block(p, b, war, mode, blk, x_ap_of_dc, x_dep, wr_sb, g8big,
                      tiles, ps, extra_deps=()):
    gates, tmp = tiles["gates"], tiles["tmp"]
    plog = ps["log"]
    red4 = tiles["red4"]
    _pk = ps.get("keys", {})
    kk = lambda t: _pk.get("log") if t is plog else (
        _pk.get("tr") if t is ps["tr"] else ("ps", id(t)))
    mml = b.step("pe", lambda t: [
        (t.matmul(plog[:], x_ap_of_dc(dc), wr_sb[:, dc * 8:(dc + 1) * 8],
                  start=(dc == 0), stop=(dc == NCH - 1)),
         1 if dc == NCH - 1 else 0)
        for dc in range(NCH)], deps=[x_dep] + list(extra_deps) + war.w(kk(plog)))
    if mode == "softmax":
        nmx = b.step("dve", lambda v: v.tensor_reduce(
            red4[:, 0:1], plog[:], axis=AX, op=ALU.max, negate=True),
            deps=[mml] + war.w(("red4",)))
        exs = b.step("act", lambda a: a.activation(gates[:, 0:8], plog[:], AF.Exp,
                                                   bias=red4[:, 0:1]),
                     deps=[nmx] + war.w(("gates",)))
        war.r(kk(plog), exs)
        den = b.step("dve", lambda v: v.reduce_sum(red4[:, 1:2], gates[:, 0:8], axis=AX),
                     deps=[exs])
        rc1 = b.step("act", lambda a: a.activation(red4[:, 2:3], red4[:, 1:2], AF.Ln),
                     deps=[den])
        rc2 = b.step("act", lambda a: a.activation(red4[:, 2:3], red4[:, 2:3], AF.Exp,
                                                   scale=-1.0), deps=[rc1])
        prb = b.step("dve", lambda v: v.tensor_scalar(
            gates[:, 0:8], gates[:, 0:8], red4[:, 2:3], None, ALU.mult), deps=[rc2])
    else:
        exs = b.step("act", lambda a: a.activation(gates[:, 0:8], plog[:], AF.Sigmoid),
                     deps=[mml] + war.w(("gates",)))
        war.r(kk(plog), exs)
        den = b.step("dve", lambda v: v.reduce_sum(red4[:, 1:2], gates[:, 0:8], axis=AX),
                     deps=[exs])
        rc1 = b.step("act", lambda a: a.activation(red4[:, 2:3], red4[:, 1:2], AF.Ln),
                     deps=[den])
        rc2 = b.step("act", lambda a: a.activation(red4[:, 2:3], red4[:, 2:3], AF.Exp,
                                                   scale=-1.0), deps=[rc1])
        prb = rc2
    t1_ = b.step("dve", lambda v: v.tensor_reduce(red4[:, 0:1], gates[:, 0:8], axis=AX,
                                                  op=ALU.max), deps=[prb])
    t2_ = b.step("dve", lambda v: v.tensor_scalar(tmp[:, 0:8], gates[:, 0:8],
                                                  red4[:, 0:1], None, ALU.is_ge),
                 deps=[t1_] + war.w(("tmp",)))
    t3_ = b.step("dve", lambda v: v.scalar_tensor_tensor(
        tmp[:, 0:8], tmp[:, 0:8], -2.0, gates[:, 0:8], ALU.mult, ALU.add), deps=[t2_])
    t4_ = b.step("dve", lambda v: v.tensor_reduce(red4[:, 3:4], tmp[:, 0:8], axis=AX,
                                                  op=ALU.max), deps=[t3_])
    top = b.step("dve", lambda v: v.tensor_scalar(tmp[:, 0:8], gates[:, 0:8],
                                                  red4[:, 3:4], None, ALU.is_ge), deps=[t4_])
    if mode == "softmax":
        fin = b.step("dve", lambda v: v.tensor_tensor(
            gates[:, 0:8], gates[:, 0:8], tmp[:, 0:8], ALU.mult), deps=[top])
    else:
        f1 = b.step("dve", lambda v: v.tensor_tensor(gates[:, 0:8], gates[:, 0:8],
                                                     tmp[:, 0:8], ALU.mult), deps=[top])
        fin = b.step("dve", lambda v: v.tensor_scalar(gates[:, 0:8], gates[:, 0:8],
                                                      red4[:, 2:3], None, ALU.mult),
                     deps=[f1])
    war.r(("red4",), fin)
    war.r(("tmp",), fin)
    tr = b.step("pe", lambda t: t.transpose(ps["tr"][:], gates[:],
                                            p.const["ident_f"][:]),
                deps=[fin] + war.w(kk(ps["tr"])))
    war.r(("gates",), tr)
    ev = b.step("dve", lambda v: v.tensor_copy(g8big[0:8, :], ps["tr"][0:8, :]),
                deps=[tr] + war.w(("g8st", id(g8big))))
    war.r(kk(ps["tr"]), ev)
    return ev, mml


def emit_l0_ffn(p, xfm_in, xfm_out, wi_scr, wo_scr, wr_ap, T, tag="l0f"):
    nc = p.nc
    ngrp = T // 512
    g8_dram = nc.dram_tensor(tag + "_g8d", [8, T], BF16, kind="Internal").ap()
    with contextlib.ExitStack() as ctx:
        sb = lambda n, s, d: ctx.enter_context(nc.sbuf_tensor(n, s, d))
        psum = lambda n, s, d=F32: ctx.enter_context(nc.psum_tensor(n, s, d))
        xbuf = [sb(f"l0f_x{i}", [128, NCH * 512], BF16) for i in range(2)]
        wrsb = sb("l0f_wr", [128, NCH * 8], BF16)
        g8st = [sb(f"l0f_g8st{i}", [8, 128], BF16) for i in range(2)]
        grow = [sb(f"l0f_grow{i}", [1, 512], BF16) for i in range(2)]
        gb = [sb(f"l0f_gb{i}", [128, 512], BF16) for i in range(2)]
        wibuf = [sb(f"l0f_wi{i}", [128, NCH * 1024], BF16) for i in range(2)]
        wobuf = [sb(f"l0f_wo{i}", [128, 8 * 128], BF16) for i in range(16)]
        heall = [sb(f"l0f_he{e}", [128, NCH * 512], BF16) for e in range(8)]
        rtiles = dict(gates=sb("l0f_gts", [128, 128], F32),
                      tmp=sb("l0f_tmp", [128, 8], F32),
                      red4=sb("l0f_red4", [128, 4], F32))
        ln = mk_ln_tiles(ctx, nc, "l0f")
        stage = [sb(f"l0f_stg{i}", [128, NCH * 512], BF16) for i in range(2)]

        b = p.block("l0f_router")
        war = War()
        b.dma("pool", lambda g: [
            (g.dma_start(wrsb[:, dc * 8:(dc + 1) * 8], wr_ap[dc * 128:(dc + 1) * 128, :]), 16)
            for dc in range(NCH)], n=NCH)
        wld = b.steps[-1]
        zg = b.step("pool", lambda g: g.memset(rtiles["gates"][:], 0.0))
        with nc.psum_tensor("l0f_plog", [128, 8], F32) as plog, \
                nc.psum_tensor("l0f_ptr", [128, 128], F32) as ptr:
            ps = {"log": plog, "tr": ptr}
            for gi in range(ngrp):
                slot = gi % 2
                ld = b.dma("sp", lambda s, gi=gi, slot=slot: [
                    (s.dma_start(xbuf[slot][:, dc * 512:(dc + 1) * 512],
                                 xfm_in[dc, :, gi * 512:(gi + 1) * 512]), 16)
                    for dc in range(NCH)], deps=war.w(("x", slot)), n=NCH)
                for tb in range(4):
                    blk = gi * 4 + tb
                    x_of = (lambda slot, tb: lambda dc:
                            xbuf[slot][:, dc * 512 + tb * 128: dc * 512 + tb * 128 + 128]
                            )(slot, tb)
                    ev, mml = emit_router_block(p, b, war, "softmax", blk, x_of,
                                                ld, wrsb, g8st[blk % 2], rtiles, ps,
                                                extra_deps=[wld, zg])
                    war.r(("x", slot), mml)
                    g8w = b.dma("sp", lambda s, blk=blk: s.dma_start(
                        g8_dram[0:8, blk * 128:(blk + 1) * 128], g8st[blk % 2][:]),
                        deps=[ev])
                    war.r(("g8st", id(g8st[blk % 2])), g8w)
            b.step("sp", lambda s: s.nop(), deps=[g8w])
            b.run()

        b = p.block("l0f_main")
        war = War()
        phe = [psum(f"l0f_phe{i}", [128, 512]) for i in range(2)]
        pout = [psum(f"l0f_pout{i}", [128, 512]) for i in range(2)]
        st_last = None
        for gi in range(ngrp):
            slot = gi % 2
            ld = b.dma("sp", lambda s, gi=gi, slot=slot: [
                (s.dma_start(xbuf[slot][:, dc * 512:(dc + 1) * 512],
                             xfm_in[dc, :, gi * 512:(gi + 1) * 512]), 16)
                for dc in range(NCH)], deps=war.w(("x", slot)), n=NCH)
            for e in range(8):
                wslot = e % 2
                rw = b.dma("sp", lambda s, gi=gi, e=e: s.dma_start(
                    grow[e % 2][:], g8_dram[e:e + 1, gi * 512:(gi + 1) * 512]),
                    deps=war.w(("grow", e % 2)))
                wl = b.dma("pool", lambda g, e=e, wslot=wslot: [
                    (g.dma_start(wibuf[wslot][:, dc * 1024:(dc + 1) * 1024],
                                 wi_scr[e, dc * 128:(dc + 1) * 128, :]), 16)
                    for dc in range(NCH)], deps=war.w(("wi", wslot)), n=NCH)
                gbs = e % 2
                bcm = b.step("pe", lambda t, e=e, gbs=gbs: t.matmul(
                    pout[gbs][:], p.const["ones_row_b"][:],
                    grow[e % 2][:], start=True, stop=True),
                    deps=[rw] + war.w(("pout", gbs)))
                war.r(("grow", e % 2), bcm)
                gbe = b.step("act", lambda a, gbs=gbs: a.activation(
                    gb[gbs][:], pout[gbs][:], AF.Identity),
                    deps=[bcm] + war.w(("gb", gbs)))
                war.r(("pout", gbs), gbe)
                for hc in range(NCH):
                    j = e * NCH + hc
                    r = j % 2
                    mmh = b.step("pe", lambda t, hc=hc, slot=slot, wslot=wslot, r=r: [
                        (t.matmul(phe[r][:],
                                  wibuf[wslot][:, dc * 1024 + hc * 128: dc * 1024 + hc * 128 + 128],
                                  xbuf[slot][:, dc * 512:(dc + 1) * 512],
                                  start=(dc == 0), stop=(dc == NCH - 1)),
                         1 if dc == NCH - 1 else 0)
                        for dc in range(NCH)],
                        deps=[ld, wl] + war.w(("phe", r)))
                    war.r(("x", slot), mmh)
                    war.r(("wi", wslot), mmh)
                    gev = b.step("dve", lambda v, e=e, hc=hc, r=r, gbs=gbs: v.tensor_tensor(
                        heall[e][:, hc * 512:(hc + 1) * 512], phe[r][:], gb[gbs][:],
                        ALU.mult), deps=[gbe, mmh] + war.w(("he", e, hc)))
                    war.r(("phe", r), gev)
                    war.r(("gb", gbs), gev)
            hev = gev
            for co in range(NCH):
                r = co % 2
                wo_ld = {}
                for e in range(8):
                    ws = (co % 2) * 8 + e
                    wo_ld[e] = b.dma("pool", lambda g, co=co, e=e, ws=ws:
                                     g.dma_start(wobuf[ws][:], wo_scr[e, co, :, :]),
                                     deps=war.w(("wo", ws)))
                def mmo_fn(t, co=co, r=r):
                    out = []
                    for e in range(8):
                        ws = (co % 2) * 8 + e
                        for hc in range(NCH):
                            out.append((t.matmul(
                                pout[r][:], wobuf[ws][:, hc * 128:(hc + 1) * 128],
                                heall[e][:, hc * 512:(hc + 1) * 512],
                                start=(e == 0 and hc == 0), stop=(e == 7 and hc == NCH - 1)),
                                1 if (e == 7 and hc == NCH - 1) else 0))
                    return out
                mmo = b.step("pe", mmo_fn,
                             deps=[hev] + list(wo_ld.values()) + war.w(("pout", r)))
                for e in range(8):
                    war.r(("wo", (co % 2) * 8 + e), mmo)
                    for hc in range(NCH):
                        war.r(("he", e, hc), mmo)
                rcs_a = b.step("dve", lambda v, co=co, r=r, slot=slot: v.tensor_tensor(
                    ln["rbuf"][:, co * 512:(co + 1) * 512], pout[r][:],
                    xbuf[slot][:, co * 512:(co + 1) * 512], ALU.add),
                    deps=[mmo] + war.w(("rb", co)))
                rcs = b.step("dve", lambda v, co=co: v.tensor_copy(
                    ln["rbb"][:, co * 512:(co + 1) * 512],
                    ln["rbuf"][:, co * 512:(co + 1) * 512]), deps=[rcs_a])
                war.r(("pout", r), rcs)
                war.r(("x", slot), rcs)
            out_step = emit_ln(p, b, ln, war, rcs, p.const["eps_ln"],
                               stage[gi % 2], war.w(("stg", gi % 2)))
            st_last = b.dma("sp", lambda s, gi=gi: [
                (s.dma_start(xfm_out[c, :, gi * 512:(gi + 1) * 512],
                             stage[gi % 2][:, c * 512:(c + 1) * 512]), 16)
                for c in range(NCH)], deps=[out_step], n=NCH)
            war.r(("stg", gi % 2), st_last)
        b.step("sp", lambda s: s.nop(), deps=[st_last])
        b.run()


def emit_l1_attn(p, xfm_in, xfm_out, W, nb, seq):
    nc = p.nc
    C = p.const
    nqc = seq // 512
    with contextlib.ExitStack() as ctx:
        sb = lambda n, s, d: ctx.enter_context(nc.sbuf_tensor(n, s, d))
        psum = lambda n, s, d=F32: ctx.enter_context(nc.psum_tensor(n, s, d))
        wbuf = [sb(f"l1a_w{i}", [128, NCH * 1024], BF16) for i in range(2)]
        x2buf = [sb(f"l1a_x{i}", [128, NCH * 512], BF16) for i in range(2)]
        kfm = sb("l1a_kfm", [128, NCH * seq], BF16)
        vtok = sb("l1a_vtok", [128, (seq // 128) * 1024], BF16)
        qbuf = [sb("l1a_q0", [128, NCH * 512], BF16)] * 2
        eslab = sb("l1a_eslab", [128, (seq // 128) * 512], BF16)
        obuf = [sb("l1a_o0", [128, NCH * 512], BF16)] * 2
        wrow = [sb(f"l1a_wr{i}", [1, 512], F32) for i in range(2)]
        wb = [sb(f"l1a_wb{i}", [128, 512], F32) for i in range(2)]
        ln = mk_ln_tiles(ctx, nc, "l1a", ext_psum=True)
        stage = [sb("l1a_stg0", [128, NCH * 512], BF16)] * 2
        pb = [psum(f"l1a_pb{i}", [128, 512]) for i in range(6)]
        pr = [psum(f"l1a_pr{i}", [1, 512]) for i in range(2)]
        ln.update(pS1=pb[0][0:1, :], pS2=pb[1][0:1, :], pB1=pb[0][:], pB2=pb[1][:])
        bk = lambda t: ("ps", id(t))
        ln["_k"] = dict(pS1=bk(pb[0]), pS2=bk(pb[1]), pB1=bk(pb[0]), pB2=bk(pb[1]))

        b = p.block("l1a_main")
        war = War()
        st_last = None
        for bi in range(nb):
            wkv = []
            for i, k in enumerate(["Wk", "Wv"]):
                wkv.append(b.dma("pool", lambda g, k=k, i=i: [
                    (g.dma_start(wbuf[i][:, dc * 1024:(dc + 1) * 1024],
                                 W[k][dc * 128:(dc + 1) * 128, :]), 16)
                    for dc in range(NCH)], deps=war.w(("wb", i)), n=NCH))
            for tc in range(seq // 512):
                slot = tc % 2
                ld = b.dma("sp", lambda s, bi=bi, tc=tc, slot=slot: [
                    (s.dma_start(x2buf[slot][:, dc * 512:(dc + 1) * 512],
                                 xfm_in[dc, :, bi * seq + tc * 512: bi * seq + tc * 512 + 512]), 16)
                    for dc in range(NCH)], deps=war.w(("x", slot)), n=NCH)
                for c in range(NCH):
                    j = tc * NCH + c
                    r = j % 2
                    mmk = b.step("pe", lambda t, c=c, slot=slot, r=r: [
                        (t.matmul(pb[2 + r][:],
                                  wbuf[0][:, dc * 1024 + c * 128: dc * 1024 + c * 128 + 128],
                                  x2buf[slot][:, dc * 512:(dc + 1) * 512],
                                  start=(dc == 0), stop=(dc == NCH - 1)),
                         1 if dc == NCH - 1 else 0)
                        for dc in range(NCH)],
                        deps=[ld, wkv[0]] + war.w(("pst", r)))
                    war.r(("x", slot), mmk)
                    war.r(("wb", 0), mmk)
                    evk = b.step("act", lambda a, c=c, tc=tc, r=r: a.activation(
                        kfm[:, c * seq + tc * 512: c * seq + tc * 512 + 512],
                        pb[2 + r][:], AF.Identity), deps=[mmk] + war.w(("kfm", c, tc)))
                    war.r(("pst", r), evk)
                for tb4 in range(4):
                    tb = tc * 4 + tb4
                    for half in range(2):
                        j2 = tb * 2 + half
                        r = j2 % 2
                        mmv = b.step("pe", lambda t, tb=tb, tb4=tb4, half=half, slot=slot, r=r: [
                            (t.matmul(pb[2 + r][:],
                                      x2buf[slot][:, dc * 512 + tb4 * 128: dc * 512 + tb4 * 128 + 128],
                                      wbuf[1][:, dc * 1024 + half * 512: dc * 1024 + half * 512 + 512],
                                      start=(dc == 0), stop=(dc == NCH - 1)),
                             1 if dc == NCH - 1 else 0)
                            for dc in range(NCH)],
                            deps=[ld, wkv[1]] + war.w(("pst", r)))
                        war.r(("x", slot), mmv)
                        war.r(("wb", 1), mmv)
                        evv = b.step("act", lambda a, tb=tb, half=half, r=r: a.activation(
                            vtok[:, tb * 1024 + half * 512: tb * 1024 + half * 512 + 512],
                            pb[2 + r][:], AF.Identity), deps=[mmv] + war.w(("vtok", tb, half)))
                        war.r(("pst", r), evv)
            kv_done = evv
            wqo = []
            for i, k in enumerate(["Wq", "Wo"]):
                wqo.append(b.dma("pool", lambda g, k=k, i=i: [
                    (g.dma_start(wbuf[i][:, dc * 1024:(dc + 1) * 1024],
                                 W[k][dc * 128:(dc + 1) * 128, :]), 16)
                    for dc in range(NCH)], deps=war.w(("wb", i)), n=NCH))
            for qc in range(nqc):
                slot = qc % 2
                ld = b.dma("sp", lambda s, bi=bi, qc=qc, slot=slot: [
                    (s.dma_start(x2buf[slot][:, dc * 512:(dc + 1) * 512],
                                 xfm_in[dc, :, bi * seq + qc * 512: bi * seq + qc * 512 + 512]), 16)
                    for dc in range(NCH)], deps=war.w(("x", slot)), n=NCH)
                for c in range(NCH):
                    j = qc * NCH + c
                    r = j % 2
                    mmq = b.step("pe", lambda t, c=c, slot=slot, r=r: [
                        (t.matmul(pb[2 + r][:],
                                  wbuf[0][:, dc * 1024 + c * 128: dc * 1024 + c * 128 + 128],
                                  x2buf[slot][:, dc * 512:(dc + 1) * 512],
                                  start=(dc == 0), stop=(dc == NCH - 1)),
                         1 if dc == NCH - 1 else 0)
                        for dc in range(NCH)],
                        deps=[ld, wqo[0], kv_done] + war.w(("pst", r)))
                    war.r(("x", slot), mmq)
                    war.r(("wb", 0), mmq)
                    evq = b.step("act", lambda a, c=c, r=r, slot=slot: a.activation(
                        qbuf[slot][:, c * 512:(c + 1) * 512], pb[2 + r][:], AF.Identity,
                        scale=0.125), deps=[mmq] + war.w(("qb", c)))
                    war.r(("pst", r), evq)
                qdone = evq
                def normalizer(h, av_last, cs_last):
                    cc, off, hp = h // 2, (h % 2) * 64, h % 2
                    rc1 = b.step("act", lambda a, hp=hp: a.activation(
                        wrow[hp][:], pr[hp][:], AF.Ln),
                        deps=[cs_last] + war.w(("wrow", hp)))
                    war.r(("pr0", hp), rc1)
                    rc2 = b.step("act", lambda a, hp=hp: a.activation(
                        wrow[hp][:], wrow[hp][:], AF.Exp, scale=-1.0), deps=[rc1])
                    bcw = b.step("pe", lambda t, hp=hp: t.matmul(
                        pb[0][:], C["ones_row_f"][:], wrow[hp][:], start=True, stop=True),
                        deps=[rc2] + war.w(bk(pb[0])))
                    war.r(("wrow", hp), bcw)
                    evw = b.step("act", lambda a, hp=hp: a.activation(
                        wb[hp][:], pb[0][:], AF.Identity), deps=[bcw] + war.w(("wbt", hp)))
                    war.r(bk(pb[0]), evw)
                    osc = b.step("dve", lambda v, cc=cc, off=off, hp=hp, slot=slot:
                                 v.tensor_tensor(
                        obuf[slot][off:off + 64, cc * 512:(cc + 1) * 512],
                        pb[4 + hp][0:64, :], wb[hp][0:64, :], ALU.mult),
                        deps=[evw, av_last] + war.w(("ob", h)))
                    war.r(("pb4", hp), osc)
                    war.r(("wbt", hp), osc)
                    return osc

                prev_head = None
                osc = None
                for h in range(16):
                    cc, off, hp = h // 2, (h % 2) * 64, h % 2
                    nkb = 4 * qc + 4
                    pend = []
                    cs = av = None
                    for kb in range(nkb + 1):
                        if kb < nkb:
                            qoff = max(0, (kb - 4 * qc) * 128)
                            r = kb % 2
                            mmst = b.step("pe", lambda t, cc=cc, off=off, kb=kb,
                                          qoff=qoff, slot=slot, r=r: t.matmul(
                                pb[2 + r][0:128, qoff:512],
                                kfm[off:off + 64, cc * seq + kb * 128: cc * seq + kb * 128 + 128],
                                qbuf[slot][off:off + 64, cc * 512 + qoff: cc * 512 + 512],
                                start=True, stop=True),
                                deps=[qdone, kv_done] + war.w(("pst", r)))
                            war.r(("qb", cc), mmst)
                            eve = b.step("act", lambda a, kb=kb, qoff=qoff, r=r:
                                         a.activation(
                                eslab[:, kb * 512 + qoff: kb * 512 + 512],
                                pb[2 + r][0:128, qoff:512], AF.Exp),
                                deps=[mmst] + war.w(("esl", kb)))
                            war.r(("pst", r), eve)
                            pend.append((kb, qoff, eve))
                        while pend and (kb == nkb or len(pend) > 2):
                            kb2, qoff2, eve2 = pend.pop(0)
                            cs = b.step("pe", lambda t, kb=kb2, qoff=qoff2, nkb=nkb,
                                        hp=hp: t.matmul(
                                pr[hp][0:1, qoff:512], C["ones_col_b"][:],
                                eslab[:, kb * 512 + qoff: kb * 512 + 512],
                                start=(kb == 0), stop=(kb == nkb - 1),
                                skip_group_check=True),
                                deps=[eve2] + (war.w(("pr0", hp)) if kb2 == 0 else []))
                            av = b.step("pe", lambda t, kb=kb2, qoff=qoff2, h=h,
                                        nkb=nkb, hp=hp: t.matmul(
                                pb[4 + hp][0:64, qoff:512],
                                vtok[:, kb * 1024 + h * 64: kb * 1024 + h * 64 + 64],
                                eslab[:, kb * 512 + qoff: kb * 512 + 512],
                                start=(kb == 0), stop=(kb == nkb - 1),
                                skip_group_check=True),
                                deps=[eve2] + (war.w(("pb4", hp)) if kb2 == 0 else []))
                            war.r(("esl", kb2), av)
                            war.r(("vtok", kb2, 0), av)
                            war.r(("vtok", kb2, 1), av)
                    assert not pend
                    if prev_head is not None:
                        osc = normalizer(*prev_head)
                    prev_head = (h, av, cs)
                osc = normalizer(*prev_head)
                odone = osc

                rcs = None
                for co in range(NCH):
                    j = qc * NCH + co
                    r = j % 2
                    mmo = b.step("pe", lambda t, co=co, slot=slot, r=r: [
                        (t.matmul(pb[2 + r][:],
                                  wbuf[1][:, cc2 * 1024 + co * 128: cc2 * 1024 + co * 128 + 128],
                                  obuf[slot][:, cc2 * 512:(cc2 + 1) * 512],
                                  start=(cc2 == 0), stop=(cc2 == NCH - 1)),
                         1 if cc2 == NCH - 1 else 0)
                        for cc2 in range(NCH)],
                        deps=[odone, wqo[1]] + war.w(("pst", r)))
                    war.r(("wb", 1), mmo)
                    for h2 in range(16):
                        war.r(("ob", h2), mmo)
                    rcs_a = b.step("dve", lambda v, co=co, r=r, slot=slot: v.tensor_tensor(
                        ln["rbuf"][:, co * 512:(co + 1) * 512], pb[2 + r][:],
                        x2buf[slot][:, co * 512:(co + 1) * 512], ALU.add),
                        deps=[mmo] + war.w(("rb", co)))
                    war.r(("pst", r), rcs_a)
                    war.r(("x", slot), rcs_a)
                    rcs = b.step("dve", lambda v, co=co: v.tensor_copy(
                        ln["rbb"][:, co * 512:(co + 1) * 512],
                        ln["rbuf"][:, co * 512:(co + 1) * 512]), deps=[rcs_a])
                out_step = emit_ln(p, b, ln, war, rcs, p.const["eps_ln"],
                                   stage[0], war.w(("stg",)))
                st_last = b.dma("sp", lambda s, bi=bi, qc=qc: [
                    (s.dma_start(xfm_out[c, :, bi * seq + qc * 512: bi * seq + qc * 512 + 512],
                                 stage[0][:, c * 512:(c + 1) * 512]), 16)
                    for c in range(NCH)], deps=[out_step], n=NCH)
                war.r(("stg",), st_last)
        b.step("sp", lambda s: s.nop(), deps=[st_last])
        b.run()


def emit_l1_ffn(p, xfm_in, out_ap, w1_scr, w2_scr, b1_list, b2r_ap, b2s_ap,
                cent_ap, nb, seq):
    nc = p.nc
    C = p.const
    T = nb * seq
    ngrp = T // 1024
    NE = 10
    EXPERTS = [8, 9] + list(range(8))
    g8_dram = nc.dram_tensor("l1f_g8d", [8, T], BF16, kind="Internal").ap()
    with contextlib.ExitStack() as ctx:
        sb = lambda n, s, d: ctx.enter_context(nc.sbuf_tensor(n, s, d))
        wb = sb("l1f_wb", [128, 16384], BF16)
        bigA = sb("l1f_bigA", [128, 32768], BF16)
        ubuf = [sb("l1f_u0", [128, 8192], BF16)] * 2
        bigB = sb("l1f_bigB", [128, 8192], BF16)
        acc = sb("l1f_acc", [128, NCH * 1024], F32)
        ln = mk_ln_tiles(ctx, nc, "l1f", ext_psum=True, rbb=False, sq_dtype=BF16)
        rstdb = [sb(f"l1f_rstdb{i}", [128, 512], F32) for i in range(2)]
        gbt = sb("l1f_gbt", [128, 1024], BF16)
        spt = sb("l1f_spt", [128, 512], BF16)
        grow = [sb("l1f_grow0", [1, 1024], BF16)] * 2
        wrow10 = sb("l1f_wrow10", [16, 1024], BF16)
        b1sb = sb("l1f_b1", [128, NE * 32], F32)
        b2sb = sb("l1f_b2", [16, 1024], BF16)
        centsb = sb("l1f_cent", [128, NCH * 8], BF16)
        g8st = [sb(f"l1f_g8st{i}", [8, 128], BF16) for i in range(2)]
        rtiles = dict(gates=sb("l1f_gts", [128, 128], F32),
                      tmp=sb("l1f_tmp", [128, 8], F32),
                      red4=sb("l1f_red4", [128, 4], F32))
        outst = [bigB.bitcast(F32)[:, 2048:3072]] * 2
        pool8 = [ctx.enter_context(nc.psum_tensor(f"l1f_pp{i}", [128, 512], F32))
                 for i in range(8)]
        ln.update(pS1=pool8[4][0:1, :], pS2=pool8[5][0:1, :],
                  pB1=pool8[6][:], pB2=pool8[7][:])
        bk = lambda t: ("ps", id(t))
        kS1, kS2, kB1, kB2 = bk(pool8[4]), bk(pool8[5]), bk(pool8[6]), bk(pool8[7])
        ln["_k"] = dict(pS1=kS1, pS2=kS2, pB1=kB1, pB2=kB2)

        u_of = lambda up, dc: ubuf[up][:, dc * 1024:(dc + 1) * 1024]
        xn_of = lambda dc: bigB[:, dc * 1024:(dc + 1) * 1024]
        h_of = lambda hc: bigA[:, hc * 1024:(hc + 1) * 1024]
        w1t = lambda dc, hcin: wb[:, dc * 2048 + hcin * 128: dc * 2048 + hcin * 128 + 128]
        w2t = lambda hcin, c: wb[:, hcin * 1024 + c * 128: hcin * 1024 + c * 128 + 128]

        b = p.block("l1f_pre")
        b.dma("pool", lambda g: [
            (g.dma_start(centsb[:, dc * 8:(dc + 1) * 8],
                         cent_ap[dc * 128:(dc + 1) * 128, :]), 16)
            for dc in range(NCH)], n=NCH)
        for e in range(NE):
            load_bias_cols(b, b1sb[:, e * 32:(e + 1) * 32], b1_list[e], 32)
        b.dma("pool", lambda g: g.dma_start(b2sb[0:8, :], b2r_ap[:, :]))
        b.dma("pool", lambda g: g.dma_start(b2sb[8:10, :], b2s_ap[:, :]))
        b.step("pool", lambda g: g.memset(wrow10[:], 1.0))
        b.step("pool", lambda g: g.memset(rtiles["gates"][:], 0.0))
        b.run()

        b = p.block("l1f_main")
        war = War()
        st_last = None

        def prep(gi):
            up = 0
            uld = b.dma("sp", lambda s, gi=gi, up=up: [
                (s.dma_start(u_of(up, dc), xfm_in[dc, :, gi * 1024:(gi + 1) * 1024]), 16)
                for dc in range(NCH)], deps=war.w(("u", up)), n=NCH)
            g8w = None
            for tb in range(8):
                x_of = (lambda tb, up: lambda dc:
                        u_of(up, dc)[:, tb * 128:(tb + 1) * 128])(tb, up)
                ev, mml = emit_router_block(
                    p, b, war, "sigmoid", tb, x_of, uld, centsb, g8st[tb % 2],
                    rtiles, {"log": pool8[0][:, 0:8], "tr": pool8[6][:, 0:128],
                             "keys": {"log": bk(pool8[0]), "tr": kB1}})
                war.r(("u", up), mml)
                g8w = b.dma("sp", lambda s, gi=gi, tb=tb: s.dma_start(
                    g8_dram[0:8, gi * 1024 + tb * 128: gi * 1024 + tb * 128 + 128],
                    g8st[tb % 2][:]), deps=[ev])
                war.r(("g8st", id(g8st[tb % 2])), g8w)
            evrs = []
            for tc in range(2):
                mms = None
                for dc in range(NCH):
                    sq = b.step("act", lambda a, tc=tc, dc=dc, up=up: a.activation(
                        ln["sqb"][dc % 2][:], u_of(up, dc)[:, tc * 512:(tc + 1) * 512],
                        AF.Square), deps=[uld] + war.w(("lnsq", dc % 2)))
                    war.r(("u", up), sq)
                    mms = b.step("pe", lambda t, dc=dc: t.matmul(
                        ln["pS1"], C["ones_col_b"][:], ln["sqb"][dc % 2][:],
                        start=(dc == 0), stop=(dc == NCH - 1), skip_group_check=True),
                        deps=[sq] + (war.w(kS1) if dc == 0 else []))
                    war.r(("lnsq", dc % 2), mms)
                r1 = b.step("act", lambda a: a.activation(
                    ln["s1row"][:], ln["pS1"], AF.Ln, scale=1.0 / D,
                    bias=C["eps_rms"][0:1, :]), deps=[mms] + war.w(("s1row",)))
                war.r(kS1, r1)
                r2 = b.step("act", lambda a: a.activation(
                    ln["s1row"][:], ln["s1row"][:], AF.Exp, scale=-0.5), deps=[r1])
                bcr = b.step("pe", lambda t: t.matmul(
                    pool8[7][:], C["ones_row_f"][:], ln["s1row"][:], start=True,
                    stop=True), deps=[r2] + war.w(kB2))
                war.r(("s1row",), bcr)
                evr = b.step("act", lambda a, tc=tc: a.activation(
                    rstdb[tc][:], pool8[7][:], AF.Identity),
                    deps=[bcr] + war.w(("rstdb", tc)))
                war.r(kB2, evr)
                evrs.append(evr)
            wrl = b.dma("sp", lambda s, gi=gi: s.dma_start(
                wrow10[0:8, :], g8_dram[0:8, gi * 1024:(gi + 1) * 1024]),
                deps=[g8w] + war.w(("wrow10",)))
            return dict(uld=uld, evrs=evrs, wrl=wrl)

        pstate = prep(0)
        for gi in range(ngrp):
            up = 0
            uld, evrs, wrl = pstate["uld"], pstate["evrs"], pstate["wrl"]
            xn_done = []
            for tc in range(2):
                for dc in range(NCH):
                    xs = b.step("dve", lambda v, dc=dc, tc=tc, up=up: v.tensor_tensor(
                        xn_of(dc)[:, tc * 512:(tc + 1) * 512],
                        u_of(up, dc)[:, tc * 512:(tc + 1) * 512], rstdb[tc][:],
                        ALU.mult),
                        deps=[evrs[tc], uld] + war.w(("xn", dc, tc))
                        + (war.w(("uslice",)) if dc < 4 else [])
                        + (war.w(("outst",)) if dc in (4, 5) else []))
                    war.r(("u", up), xs)
                    war.r(("rstdb", tc), xs)
                    xn_done.append(xs)
            for ei, e in enumerate(EXPERTS):
                routed = e < 8
                if routed:
                    grl = b.dma("sp", lambda s, e=e, gi=gi: s.dma_start(
                        grow[e % 2][:], g8_dram[e:e + 1, gi * 1024:(gi + 1) * 1024]),
                        deps=[pstate["wrl"]] + war.w(("grow",)))
                    gsteps = []
                    for half in range(2):
                        bcg = b.step("pe", lambda t, e=e, half=half: t.matmul(
                            pool8[7][:], C["ones_row_b"][:],
                            grow[e % 2][0:1, half * 512:(half + 1) * 512],
                            start=True, stop=True),
                            deps=[grl] + war.w(kB2))
                        evg = b.step("act", lambda a, half=half: a.activation(
                            gbt[:, half * 512:(half + 1) * 512], pool8[7][:],
                            AF.Identity),
                            deps=[bcg] + (war.w(("gbt",)) if half == 0 else []))
                        war.r(kB2, evg)
                        gsteps.append(evg)
                    war.r(("grow",), gsteps[-1])
                wh = [wb[:, 0:8192], wb[:, 8192:16384]]
                w1t_r = lambda rg, dc, hcin: wh[rg][:, dc * 1024 + hcin * 128:
                                                    dc * 1024 + hcin * 128 + 128]
                w2t_r = lambda rg, hcq, c: wh[rg][:, (hcq // 4) * 4096 +
                                                  (hcq % 4) * 1024 + c * 128:
                                                  (hcq // 4) * 4096 +
                                                  (hcq % 4) * 1024 + c * 128 + 128]
                hdeps = xn_done if ei == 0 else []
                last_h = None
                w1l = {}

                def load_w1q(q, e=e):
                    rg = q % 2
                    w1l[q] = b.dma("pool", lambda g, e=e, q=q, rg=rg: [
                        (g.dma_start(wh[rg][:, dc * 1024:(dc + 1) * 1024],
                                     w1_scr[e, dc * 128:(dc + 1) * 128,
                                            q * 1024:(q + 1) * 1024]), 16)
                        for dc in range(NCH)], deps=war.w(("wh", rg)), n=NCH)
                load_w1q(0)
                load_w1q(1)
                for q in range(4):
                    rg = q % 2
                    for tc in range(2):
                        for hcin in range(8):
                            hc = q * 8 + hcin
                            r = hc % 2
                            mmh = b.step("pe", lambda t, tc=tc, hcin=hcin, rg=rg, r=r: [
                                (t.matmul(pool8[r][:], w1t_r(rg, dc, hcin),
                                          xn_of(dc)[:, tc * 512:(tc + 1) * 512],
                                          start=(dc == 0), stop=(dc == NCH - 1)),
                                 1 if dc == NCH - 1 else 0)
                                for dc in range(NCH)],
                                deps=[w1l[q]] + hdeps + war.w(bk(pool8[r])))
                            war.r(("wh", rg), mmh)
                            for dc in range(NCH):
                                war.r(("xn", dc, tc), mmh)
                            gel = b.step("act", lambda a, e=e, hc=hc, tc=tc, r=r:
                                         a.activation(
                                h_of(hc)[:, tc * 512:(tc + 1) * 512], pool8[r][:],
                                AF.Gelu, bias=b1sb[:, e * 32 + hc:e * 32 + hc + 1]),
                                deps=[mmh]
                                + (war.w(("h", hc)) if tc == 0 else []))
                            war.r(bk(pool8[r]), gel)
                            last_h = gel
                    if q + 2 < 4:
                        load_w1q(q + 2)
                for tc in range(2):
                    mmw = {}
                    for q in range(4):
                        rg = q % 2
                        w2l = b.dma("pool", lambda g, e=e, q=q, rg=rg: [
                            (g.dma_start(
                                wh[rg][:, bq2 * 4096:(bq2 + 1) * 4096].rearrange(
                                    "p (s c) -> p s c", c=1024),
                                w2_scr[e, q * 1024 + bq2 * 512:
                                       q * 1024 + (bq2 + 1) * 512, :].rearrange(
                                    "(s p) c -> p s c", p=128)), 16)
                            for bq2 in range(2)],
                            deps=war.w(("wh", rg)) + ([last_h] if q == 0 else []),
                            n=2)
                        for c in range(NCH):
                            def w2fn(t, e=e, ei=ei, c=c, tc=tc, q=q, rg=rg):
                                out = []
                                if ei == 0 and q == 0:
                                    out.append((t.matmul(
                                        pool8[c][:], b2sb[0:NE, c * 128:(c + 1) * 128],
                                        wrow10[0:NE, tc * 512:(tc + 1) * 512],
                                        start=True, stop=False,
                                        skip_group_check=True), 0))
                                for hcq in range(8):
                                    out.append((t.matmul(
                                        pool8[c][:], w2t_r(rg, hcq, c),
                                        h_of(q * 8 + hcq)[:, tc * 512:(tc + 1) * 512],
                                        start=(hcq == 0 and q == 0 and ei != 0),
                                        stop=(hcq == 7 and q == 3),
                                        skip_group_check=True),
                                        1 if hcq == 7 else 0))
                                return out
                            mmw[(c, q)] = b.step(
                                "pe", w2fn,
                                deps=[w2l, last_h, wrl]
                                + (war.w(bk(pool8[c])) if q == 0 else [])
                                + [mmw.get((c, q - 1))])
                            war.r(("wh", rg), mmw[(c, q)])
                            war.r(("wrow10",), mmw[(c, q)])
                            for hcq in range(8):
                                war.r(("h", q * 8 + hcq), mmw[(c, q)])
                    for c in range(NCH):
                        aslice = acc[:, c * 1024 + tc * 512: c * 1024 + tc * 512 + 512]
                        fin = mmw[(c, 3)]
                        if ei == 0:
                            sp_ = b.step("dve", lambda v, aslice=aslice, c=c:
                                         v.tensor_copy(aslice, pool8[c][:]),
                                         deps=[fin] + war.w(("acc", c, tc)))
                        elif not routed:
                            sp_ = b.step("dve", lambda v, aslice=aslice, c=c:
                                         v.tensor_tensor(aslice, aslice, pool8[c][:],
                                                         ALU.add), deps=[fin])
                        else:
                            g1 = b.step("dve", lambda v, c=c, tc=tc: v.tensor_tensor(
                                spt[:], pool8[c][:], gbt[:, tc * 512:(tc + 1) * 512],
                                ALU.mult), deps=[fin, gsteps[-1]] + war.w(("spt",)))
                            war.r(("gbt",), g1)
                            sp_ = b.step("dve", lambda v, aslice=aslice: v.tensor_tensor(
                                aslice, aslice, spt[:], ALU.add), deps=[g1])
                            war.r(("spt",), sp_)
                        war.r(bk(pool8[c]), sp_)
                        war.r(("accdone", c, tc), sp_)
                if ei == 3 and gi + 1 < ngrp:
                    pstate = prep(gi + 1)
            for tc in range(2):
                url = b.dma("sp", lambda s, gi=gi, tc=tc: [
                    (s.dma_start(bigB[:, dc * 512:(dc + 1) * 512],
                                 xfm_in[dc, :, gi * 1024 + tc * 512:
                                        gi * 1024 + tc * 512 + 512]), 16)
                    for dc in range(NCH)],
                    deps=(sum((war.w(("xn", dc, tcx)) for dc in range(4)
                               for tcx in range(2)), []) if tc == 0 else [])
                    + war.w(("uslice",)), n=NCH)
                rcs = None
                for c in range(NCH):
                    rcs = b.step("dve", lambda v, c=c, tc=tc: v.scalar_tensor_tensor(
                        ln["rbuf"][:, c * 512:(c + 1) * 512],
                        bigB[:, c * 512:(c + 1) * 512], 2.0,
                        acc[:, c * 1024 + tc * 512: c * 1024 + tc * 512 + 512],
                        ALU.mult, ALU.add),
                        deps=[url] + war.w(("rb", c)) + war.w(("accdone", c, tc)))
                    war.r(("uslice",), rcs)
                    war.r(("acc", c, tc), rcs)
                out_step = emit_ln(p, b, ln, war, rcs, C["eps_ln"], None, [],
                                   use_rbb=False, out_inplace_f32=True)
                for blk in range(4):
                    tglob = gi * 1024 + tc * 512 + blk * 128
                    bi, s0 = tglob // seq, tglob % seq
                    evs = []
                    for c in range(NCH):
                        j = blk * NCH + c
                        r = j % 2
                        trp = b.step("pe", lambda t, c=c, blk=blk, r=r: t.transpose(
                            pool8[r][:, 0:128],
                            ln["rbuf"][:, c * 512 + blk * 128: c * 512 + blk * 128 + 128],
                            C["ident_f"][:]),
                            deps=[out_step] + war.w(bk(pool8[r])))
                        war.r(("rb", c), trp)
                        ev2 = b.step("dve", lambda v, c=c, blk=blk, r=r: v.tensor_copy(
                            outst[blk % 2][:, c * 128:(c + 1) * 128],
                            pool8[r][:, 0:128]),
                            deps=[trp] + (war.w(("outst",)) if c == 0 else []))
                        war.r(bk(pool8[r]), ev2)
                        evs.append(ev2)
                    st_last = b.dma("sp", lambda s, bi=bi, s0=s0, blk=blk: s.dma_start(
                        out_ap[bi, s0:s0 + 128, :], outst[blk % 2][:]),
                        deps=[evs[-1]])
                    war.r(("outst",), st_last)
        b.step("sp", lambda s: s.nop(), deps=[st_last])
        b.run()


NB_PER_CORE = 4
SEQ = 2048
N_CORES = 8

WKEYS = ["L0_attn_Wq", "L0_attn_bq", "L0_attn_Wk", "L0_attn_bk", "L0_attn_Wv",
         "L0_attn_bv", "L0_attn_Wo", "L0_attn_bo", "L0_ffn_Wr", "L0_ffn_Wi",
         "L0_ffn_Wo", "L1_attn_Wq", "L1_attn_Wk", "L1_attn_Wv", "L1_attn_Wo",
         "L1_ffn_g_shared", "L1_ffn_W1_s", "L1_ffn_b1_s", "L1_ffn_W2_s",
         "L1_ffn_b2_s", "L1_ffn_g_routed", "L1_ffn_W1_r", "L1_ffn_b1_r",
         "L1_ffn_W2_r", "L1_ffn_b2_r", "L1_ffn_centroids"]
WSHAPES = {
    "L0_attn_Wq": [1024, 1024], "L0_attn_bq": [1024],
    "L0_attn_Wk": [1024, 1024], "L0_attn_bk": [1024],
    "L0_attn_Wv": [1024, 1024], "L0_attn_bv": [1024],
    "L0_attn_Wo": [1024, 1024], "L0_attn_bo": [1024],
    "L0_ffn_Wr": [1024, 8], "L0_ffn_Wi": [8, 1024, 1024], "L0_ffn_Wo": [8, 1024, 1024],
    "L1_attn_Wq": [1024, 1024], "L1_attn_Wk": [1024, 1024],
    "L1_attn_Wv": [1024, 1024], "L1_attn_Wo": [1024, 1024],
    "L1_ffn_g_shared": [1024], "L1_ffn_W1_s": [2, 1024, 4096],
    "L1_ffn_b1_s": [2, 4096], "L1_ffn_W2_s": [2, 4096, 1024],
    "L1_ffn_b2_s": [2, 1024], "L1_ffn_g_routed": [1024],
    "L1_ffn_W1_r": [8, 1024, 4096], "L1_ffn_b1_r": [8, 4096],
    "L1_ffn_W2_r": [8, 4096, 1024], "L1_ffn_b2_r": [8, 1024],
    "L1_ffn_centroids": [1024, 8],
}


def build_program(nb=NB_PER_CORE, seq=SEQ):
    nc = bass.Bass("TRN2", target_bir_lowering=False, detect_race_conditions=False)
    T = nb * seq
    x_in = nc.dram_tensor("L0_attn_x", [nb, seq, D], F32, kind="ExternalInput")
    W = {k: nc.dram_tensor(k, WSHAPES[k], F32, kind="ExternalInput").ap()
         for k in WKEYS}
    out = nc.dram_tensor("out", [nb, seq, D], F32, kind="ExternalOutput")
    xfm = [nc.dram_tensor(f"xfm{i}", [NCH, 128, T], BF16, kind="Internal").ap()
           for i in range(4)]
    wi_scr = nc.dram_tensor("wi_scr", [8, 1024, 1024], BF16, kind="Internal").ap()
    wo_scr = nc.dram_tensor("wo_scr", [8, 8, 128, 1024], BF16, kind="Internal").ap()
    w1_scr = nc.dram_tensor("w1_scr", [10, 1024, 4096], BF16, kind="Internal").ap()
    w2_scr = nc.dram_tensor("w2_scr", [10, 4096, 1024], BF16, kind="Internal").ap()

    with contextlib.ExitStack() as stack:
        p = Prog(nc, stack)
        emit_constants(p)
        with nc.sbuf_tensor("g_r", [128, 8], F32) as gr_sb, \
                nc.sbuf_tensor("g_s", [128, 8], F32) as gs_sb:
            b = p.block("gains")
            load_bias_cols(b, gr_sb, W["L1_ffn_g_routed"], 8)
            load_bias_cols(b, gs_sb, W["L1_ffn_g_shared"], 8)
            b.run()
            jobs = []
            for e in range(8):
                jobs.append((W["L0_ffn_Wi"][e], wi_scr[e], "copy", None))
                jobs.append((W["L0_ffn_Wo"][e], wo_scr[e], "tile128", None))
                jobs.append((W["L1_ffn_W1_r"][e], w1_scr[e], "copy", gr_sb))
                jobs.append((W["L1_ffn_W2_r"][e], w2_scr[e], "copy", None))
            for e in range(2):
                jobs.append((W["L1_ffn_W1_s"][e], w1_scr[8 + e], "copy", gs_sb))
                jobs.append((W["L1_ffn_W2_s"][e], w2_scr[8 + e], "copy", None))
            emit_phase_t(p, x_in.ap(), xfm[0], nb, seq)
            emit_l0_attn(p, xfm[0], xfm[1],
                         {k: W["L0_attn_" + k.lower() if k[0] == "b" else "L0_attn_" + k]
                          for k in ["Wq", "bq", "Wk", "bk", "Wv", "bv", "Wo", "bo"]},
                         nb, seq, convert_jobs=jobs)
        emit_l0_ffn(p, xfm[1], xfm[2], wi_scr, wo_scr, W["L0_ffn_Wr"], nb * seq)
        emit_l1_attn(p, xfm[2], xfm[3],
                     {k: W["L1_attn_" + k] for k in ["Wq", "Wk", "Wv", "Wo"]},
                     nb, seq)
        b1_list = [W["L1_ffn_b1_r"][e] for e in range(8)] + \
                  [W["L1_ffn_b1_s"][e] for e in range(2)]
        emit_l1_ffn(p, xfm[3], out.ap(), w1_scr, w2_scr, b1_list,
                    W["L1_ffn_b2_r"], W["L1_ffn_b2_s"], W["L1_ffn_centroids"],
                    nb, seq)
    return nc


_NC_CACHE = {}


def kernel(**inputs):
    key = "full"
    if key not in _NC_CACHE:
        _NC_CACHE[key] = build_program()
    nc = _NC_CACHE[key]
    x = np.ascontiguousarray(inputs["L0_attn_x"])
    shards = np.split(x, N_CORES, axis=0)
    in_maps = []
    for c in range(N_CORES):
        m = {"L0_attn_x": np.ascontiguousarray(shards[c])}
        for k in WKEYS:
            m[k] = np.ascontiguousarray(inputs[k])
        in_maps.append(m)
    res = run_bass_kernel_spmd(nc, in_maps, core_ids=list(range(N_CORES)))
    return np.concatenate([res.results[c]["out"] for c in range(N_CORES)], axis=0)


# version 06
# speedup vs baseline: 1.0713x; 1.0713x over previous
import contextlib
import os
import numpy as np
import ml_dtypes
import concourse.bass as bass
import concourse.mybir as mybir
from concourse.bass_utils import run_bass_kernel_spmd

F32 = mybir.dt.float32
BF16 = mybir.dt.bfloat16
I32 = mybir.dt.int32
AF = mybir.ActivationFunctionType
ALU = mybir.AluOpType
AX = mybir.AxisListType.X

D = 1024
NCH = D // 128
H_FFN = 4096
LN_EPS = 1e-5
RMS_EPS = float(np.finfo(np.float32).eps)

ENGINES = ("sp", "pe", "dve", "act", "pool")
_SECTION = {"sp": "sync", "pe": "tensor", "dve": "vector", "act": "scalar", "pool": "gpsimd"}


class Step:
    __slots__ = ("eng", "sem", "fn", "deps", "inc", "cum")

    def __init__(self, eng, sem, fn, deps, inc, cum):
        self.eng, self.sem, self.fn = eng, sem, fn
        self.deps, self.inc, self.cum = deps, inc, cum


class Prog:

    def __init__(self, nc, stack):
        self.nc = nc
        self.stack = stack
        self.sems = {}
        self.cum = {}
        for e in ENGINES:
            self.sems[e] = stack.enter_context(nc.semaphore("tk_" + e))
            self.cum[e] = 0
        for e in ("sp", "pool", "act", "dve"):
            self.sems[e + "~d"] = stack.enter_context(nc.semaphore("td_" + e))
            self.cum[e + "~d"] = 0
        self.const = {}

    def block(self, name=""):
        return Blk(self, name)


class Blk:

    def __init__(self, p, name):
        self.p = p
        self.name = name
        self.steps = []

    def step(self, eng, fn, deps=(), inc=1, _sem=None):
        sem = _sem or eng
        self.p.cum[sem] += inc
        s = Step(eng, sem, fn, tuple(d for d in deps if d is not None), inc,
                 self.p.cum[sem])
        self.steps.append(s)
        return s

    def dma(self, eng, fn, deps=(), n=1):
        return self.step(eng, fn, deps, inc=16 * n, _sem=eng + "~d")

    def run(self):
        p = self.p
        with p.nc.Block(self.name or None) as blk:
            for ename in ENGINES:
                steps = [s for s in self.steps if s.eng == ename]
                if not steps:
                    continue
                section = getattr(blk, _SECTION[ename])

                def body(eng, steps=steps, ename=ename):
                    for s in steps:
                        waits = {}
                        for d in s.deps:
                            if d.eng == ename and d.sem == ename:
                                waits[d.sem] = max(waits.get(d.sem, 0), d.cum)
                            else:
                                waits[d.sem] = max(waits.get(d.sem, 0), d.cum)
                        for sm, v in waits.items():
                            eng.wait_ge(p.sems[sm], v)
                        out = s.fn(eng)
                        sem = p.sems[s.sem]
                        if isinstance(out, list):
                            tot = 0
                            for inst, n in out:
                                if n:
                                    inst.then_inc(sem, n)
                                tot += n
                            assert tot == s.inc, f"step inc mismatch {tot} != {s.inc}"
                        else:
                            out.then_inc(sem, s.inc)

                section(body)
        self.steps = None


def emit_constants(p):
    nc = p.nc
    C = p.const
    C["ident_b"] = p.stack.enter_context(nc.sbuf_tensor("ident_b", [128, 128], BF16))
    C["ident_f"] = p.stack.enter_context(nc.sbuf_tensor("ident_f", [128, 128], F32))
    C["ones_col_b"] = p.stack.enter_context(nc.sbuf_tensor("ones_col_b", [128, 1], BF16))
    C["ones_col_f"] = p.stack.enter_context(nc.sbuf_tensor("ones_col_f", [128, 1], F32))
    C["ones_row_b"] = p.stack.enter_context(nc.sbuf_tensor("ones_row_b", [1, 128], BF16))
    C["ones_row_f"] = p.stack.enter_context(nc.sbuf_tensor("ones_row_f", [1, 128], F32))
    C["eps_ln"] = p.stack.enter_context(nc.sbuf_tensor("eps_ln", [128, 1], F32))
    C["eps_rms"] = p.stack.enter_context(nc.sbuf_tensor("eps_rms", [128, 1], F32))
    with nc.sbuf_tensor("ii1", [128, 128], I32) as ii1, \
            nc.sbuf_tensor("ii2", [128, 128], I32) as ii2:
        b = p.block("consts")
        s0 = b.step("pool", lambda g: [
            (g.iota(ii1[:], [[1, 128]], channel_multiplier=0), 0),
            (g.iota(ii2[:], [[0, 128]], channel_multiplier=1), 0),
            (g.memset(C["ones_col_b"][:], 1.0), 0),
            (g.memset(C["ones_col_f"][:], 1.0), 0),
            (g.memset(C["ones_row_b"][:], 1.0), 0),
            (g.memset(C["ones_row_f"][:], 1.0), 0),
            (g.memset(C["eps_ln"][:], LN_EPS), 0),
            (g.memset(C["eps_rms"][:], RMS_EPS), 1),
        ], inc=1)
        s1 = b.step("dve", lambda v: v.tensor_tensor(ii1[:], ii1[:], ii2[:],
                                                     ALU.is_equal), deps=[s0])
        s2 = b.step("dve", lambda v: v.tensor_copy(C["ident_b"][:], ii1[:]), deps=[s1])
        b.step("dve", lambda v: v.tensor_copy(C["ident_f"][:], ii1[:]), deps=[s2])
        b.run()


def load_bias_cols(b, dst, src_ap, n, deps=()):
    def fn(s):
        out = []
        for c in range(n):
            out.append((s.dma_start(dst[:, c:c + 1],
                                    src_ap[c * 128:(c + 1) * 128].unsqueeze(1),
                                    allow_slow_non_contiguous=True), 16))
        return out
    return b.dma("sp", fn, deps=deps, n=n)


def emit_phase_t(p, x_ap, xfm_ap, nb, seq):
    nc = p.nc
    ident_b = p.const["ident_b"]
    ngrp = nb * seq // 512
    with contextlib.ExitStack() as ctx:
        xtok = [ctx.enter_context(nc.sbuf_tensor(f"pt_xtok{i}", [128, 1024], BF16))
                for i in range(8)]
        stage = [ctx.enter_context(nc.sbuf_tensor(f"pt_stage{i}", [128, NCH * 512], BF16))
                 for i in range(2)]
        ps = [ctx.enter_context(nc.psum_tensor(f"pt_ps{i}", [128, 128], BF16))
              for i in range(4)]
        b = p.block("phase_t")
        tload = {}
        ttrans = {}
        tevac = {}
        tstore = {}
        for gi in range(ngrp):
            for tb in range(4):
                i = gi * 4 + tb
                tok0 = gi * 512 + tb * 128
                bi, s0 = tok0 // seq, tok0 % seq
                dep = ttrans.get((i - 8, NCH - 1))
                tload[i] = b.dma("pool", lambda g, i=i, bi=bi, s0=s0:
                                 g.dma_start(xtok[i % 8][:], x_ap[bi, s0:s0 + 128, :]),
                                 deps=[dep])
                for c in range(NCH):
                    j = i * 8 + c
                    deps = [tload[i], tevac.get(divmod(j - 4, 8))]
                    ttrans[(i, c)] = b.step(
                        "pe", lambda t, i=i, c=c, j=j:
                        t.transpose(ps[j % 4][:], xtok[i % 8][:, c * 128:(c + 1) * 128],
                                    ident_b[:]),
                        deps=deps)
                    gdep = tstore.get(gi - 2) if (tb == 0 and c == 0) else None
                    tevac[(i, c)] = b.step(
                        "dve", lambda v, gi=gi, tb=tb, c=c, j=j:
                        v.tensor_copy(stage[gi % 2][:, c * 512 + tb * 128:
                                                    c * 512 + tb * 128 + 128], ps[j % 4][:]),
                        deps=[ttrans[(i, c)], gdep])
            def store_fn(s, gi=gi):
                return [(s.dma_start(xfm_ap[c, :, gi * 512:(gi + 1) * 512],
                                     stage[gi % 2][:, c * 512:(c + 1) * 512]), 16)
                        for c in range(NCH)]
            tstore[gi] = b.dma("sp", store_fn,
                               deps=[tevac[(gi * 4 + 3, c)] for c in range(NCH)], n=NCH)
        last = tstore[ngrp - 1]
        b.step("sp", lambda s: s.nop(), deps=[last])
        b.run()


class War:

    def __init__(self):
        self.readers = {}

    def r(self, key, step):
        self.readers.setdefault(key, []).append(step)

    def w(self, key):
        return self.readers.pop(key, [])


def emit_l0_attn(p, xfm_in, xfm_out, W, nb, seq, convert_jobs=None):
    nc = p.nc
    ntc = seq // 512
    with contextlib.ExitStack() as ctx:
        sb = lambda n, s, d: ctx.enter_context(nc.sbuf_tensor(n, s, d))
        cvbufs = [ctx.enter_context(nc.sbuf_tensor(f"cv_b{i}", [128, 4096], BF16))
                  for i in range(4)] if convert_jobs else None
        psum = lambda n, s, d=F32: ctx.enter_context(nc.psum_tensor(n, s, d))
        wsb = {k: sb(f"l0a_{k}", [128, NCH * 1024], BF16) for k in ["wq", "wk", "wv", "wo"]}
        bsb = {k: sb(f"l0a_{k}", [128, NCH], F32) for k in ["bq", "bk", "bv", "bo"]}
        xbuf = [sb(f"l0a_x{i}", [128, NCH * 512], BF16) for i in range(2)]
        kvacc = [sb(f"l0a_kv{i}", [128, NCH], F32) for i in range(nb)]
        ksacc = [sb(f"l0a_ks{i}", [128, NCH], F32) for i in range(nb)]
        red = sb("l0a_red", [128, 2], F32)
        pre = [sb(f"l0a_pre{i}", [128, 512], F32) for i in range(2)]
        mn = [sb(f"l0a_mn{i}", [128, 512], F32) for i in range(2)]
        ex = [sb(f"l0a_ex{i}", [128, 512], F32) for i in range(2)]
        kf = [sb(f"l0a_kf{i}", [128, 512], BF16) for i in range(2)]
        vt = [sb(f"l0a_vt{i}", [128, 512], BF16) for i in range(2)]
        prod = [sb(f"l0a_prod{i}", [128, 512], BF16) for i in range(2)]
        qf = [sb(f"l0a_qf{i}", [128, 512], F32) for i in range(2)]
        t1 = [sb(f"l0a_t1{i}", [128, 512], F32) for i in range(2)]
        abuf = sb("l0a_abuf", [128, NCH * 512], BF16)
        gt = [sb(f"l0a_gt{i}", [128, 512], BF16) for i in range(2)]
        ln = mk_ln_tiles(ctx, nc, "l0a")
        stage = [sb(f"l0a_stg{i}", [128, NCH * 512], BF16) for i in range(2)]
        pMM = [psum(f"l0a_pmm{i}", [128, 512]) for i in range(4)]

        b = p.block("l0a_w")
        for k in ["wq", "wk", "wv", "wo"]:
            b.dma("pool", lambda g, k=k: [
                (g.dma_start(wsb[k][:, dc * 1024:(dc + 1) * 1024],
                             W[k.capitalize()][dc * 128:(dc + 1) * 128, :]), 16)
                for dc in range(NCH)], n=NCH)
        for k in ["bq", "bk", "bv", "bo"]:
            load_bias_cols(b, bsb[k], W[k], NCH)
        b.run()

        b = p.block("l0a_main")
        war = War()
        cv_last = None
        if convert_jobs:
            cv_last = emit_convert_steps(p, b, War(), cvbufs, convert_jobs)
        st_steps = {}
        it = 0

        def load_x(bi, tc, it):
            slot = it % 2
            step = b.dma("sp", lambda s, bi=bi, tc=tc, slot=slot: [
                (s.dma_start(xbuf[slot][:, dc * 512:(dc + 1) * 512],
                             xfm_in[dc, :, bi * seq + tc * 512: bi * seq + tc * 512 + 512]), 16)
                for dc in range(NCH)], deps=war.w(("x", slot)), n=NCH)
            return step, slot

        def proj_chain(wkey, c, slot, bank, deps):
            return b.step("pe", lambda t, wkey=wkey, c=c, slot=slot, bank=bank: [
                (t.matmul(bank[:], wsb[wkey][:, dc * 1024 + c * 128: dc * 1024 + c * 128 + 128],
                          xbuf[slot][:, dc * 512:(dc + 1) * 512],
                          start=(dc == 0), stop=(dc == NCH - 1)),
                 1 if dc == NCH - 1 else 0)
                for dc in range(NCH)], deps=deps)

        def fmap(bank, bias_key, c, r, out_tile, mm):
            prek = b.step("dve", lambda v, bank=bank, c=c, r=r: v.tensor_scalar(
                pre[r][:], bank[:], bsb[bias_key][:, c:c + 1], None, ALU.add),
                deps=[mm] + war.w(("pre", r)))
            mnk = b.step("dve", lambda v, r=r: v.tensor_scalar(
                mn[r][:], pre[r][:], 0.0, None, ALU.min),
                deps=[prek] + war.w(("mn", r)))
            exk = b.step("act", lambda a, r=r: a.activation(ex[r][:], mn[r][:], AF.Exp),
                         deps=[mnk] + war.w(("ex", r)))
            war.r(("mn", r), exk)
            fin = b.step("dve", lambda v, r=r, out_tile=out_tile: v.scalar_tensor_tensor(
                out_tile[r][:], pre[r][:], 0.0, ex[r][:], ALU.max, ALU.add),
                deps=[exk] + war.w((out_tile is kf and "kf" or "qf", r)))
            war.r(("pre", r), fin)
            war.r(("ex", r), fin)
            return prek, fin

        for bi in range(nb):
            z0 = b.step("pool", lambda g, bi=bi: [
                (g.memset(kvacc[bi][:], 0.0), 0),
                (g.memset(ksacc[bi][:], 0.0), 1)])
            war.r(("kvz", bi), z0)
            for tc in range(ntc):
                ld, slot = load_x(bi, tc, it)
                it += 1
                for c in range(NCH):
                    j = tc * NCH + c
                    r = j % 2
                    bk_, bv_ = pMM[r], pMM[2 + r]
                    mmk = proj_chain("wk", c, slot, bk_, [ld] + war.w(("bank", r)))
                    war.r(("x", slot), mmk)
                    prek, kfs = fmap(bk_, "bk", c, r, kf, mmk)
                    war.r(("bank", r), prek)
                    mmv = proj_chain("wv", c, slot, bv_, [ld] + war.w(("bank", 2 + r)))
                    war.r(("x", slot), mmv)
                    vts = b.step("act", lambda a, bv_=bv_, c=c, r=r: a.activation(
                        vt[r][:], bv_[:], AF.Identity, bias=bsb["bv"][:, c:c + 1]),
                        deps=[mmv] + war.w(("vt", r)))
                    war.r(("bank", 2 + r), vts)
                    prods = b.step("dve", lambda v, r=r: v.tensor_tensor(
                        prod[r][:], kf[r][:], vt[r][:], ALU.mult),
                        deps=[vts, kfs] + war.w(("prod", r)))
                    war.r(("vt", r), prods)
                    red_s = b.step("dve", lambda g, r=r: [
                        (g.reduce_sum(red[:, 0:1], prod[r][:], axis=AX), 0),
                        (g.reduce_sum(red[:, 1:2], kf[r][:], axis=AX), 1)],
                        deps=[prods] + war.w(("red",)))
                    acc = b.step("dve", lambda g, c=c, bi=bi: [
                        (g.tensor_tensor(kvacc[bi][:, c:c + 1], kvacc[bi][:, c:c + 1],
                                         red[:, 0:1], ALU.add), 0),
                        (g.tensor_tensor(ksacc[bi][:, c:c + 1], ksacc[bi][:, c:c + 1],
                                         red[:, 1:2], ALU.add), 1)],
                        deps=[red_s] + war.w(("kvz", bi)))
                    war.r(("red",), acc)
                    war.r(("prod", r), red_s)
                    war.r(("kf", r), red_s)
            war.readers[("kvdone", bi)] = [b.steps[-1]]

        ln_war = {}
        for bi in range(nb):
            kvd = war.readers[("kvdone", bi)][0]
            for tc in range(ntc):
                ld, slot = load_x(bi, tc, it)
                it += 1
                for c in range(NCH):
                    j = tc * NCH + c
                    r = j % 2
                    bq_ = pMM[r]
                    mmq = proj_chain("wq", c, slot, bq_, [ld, kvd] + war.w(("bank", r)))
                    war.r(("x", slot), mmq)
                    preq, qfs = fmap(bq_, "bq", c, r, qf, mmq)
                    war.r(("bank", r), preq)
                    ts1 = b.step("dve", lambda v, r=r, c=c, bi=bi: [
                        (v.tensor_scalar(t1[r][:], qf[r][:], kvacc[bi][:, c:c + 1], None, ALU.mult), 0),
                        (v.tensor_scalar(mn[r][:], qf[r][:], ksacc[bi][:, c:c + 1], 1e-6,
                                         ALU.mult, ALU.add), 1)],
                        deps=[qfs] + war.w(("t1", r)) + war.w(("mn", r)))
                    war.r(("qf", r), ts1)
                    lnq = b.step("act", lambda a, r=r: a.activation(ex[r][:], mn[r][:], AF.Ln),
                                 deps=[ts1] + war.w(("ex", r)))
                    war.r(("mn", r), lnq)
                    rec = b.step("act", lambda a, r=r: a.activation(ex[r][:], ex[r][:], AF.Exp,
                                                                   scale=-1.0), deps=[lnq])
                    aw = b.step("dve", lambda v, r=r, c=c: v.tensor_tensor(
                        abuf[:, c * 512:(c + 1) * 512], t1[r][:], ex[r][:], ALU.mult),
                        deps=[rec] + war.w(("ab", c)))
                    war.r(("t1", r), aw)
                    war.r(("ex", r), aw)
                adone = aw
                rcs_last = None
                for co in range(NCH):
                    j = tc * NCH + co
                    r = j % 2
                    bo_ = pMM[2 + r]
                    mmo = b.step("pe", lambda t, co=co, bo_=bo_: [
                        (t.matmul(bo_[:], wsb["wo"][:, cc * 1024 + co * 128: cc * 1024 + co * 128 + 128],
                                  abuf[:, cc * 512:(cc + 1) * 512],
                                  start=(cc == 0), stop=(cc == NCH - 1)),
                         1 if cc == NCH - 1 else 0)
                        for cc in range(NCH)],
                        deps=[adone] + war.w(("bank", 2 + r)))
                    for cc in range(NCH):
                        war.r(("ab", cc), mmo)
                    gls = b.step("act", lambda a, bo_=bo_, co=co, r=r: a.activation(
                        gt[r][:], bo_[:], AF.Gelu_apprx_tanh, bias=bsb["bo"][:, co:co + 1]),
                        deps=[mmo] + war.w(("gt", r)))
                    war.r(("bank", 2 + r), gls)
                    rcs_a = b.step("dve", lambda v, r=r, co=co, slot=slot: v.tensor_tensor(
                        ln["rbuf"][:, co * 512:(co + 1) * 512], gt[r][:],
                        xbuf[slot][:, co * 512:(co + 1) * 512], ALU.add),
                        deps=[gls] + war.w(("rb", co)))
                    rcs_last = b.step("dve", lambda v, co=co: v.tensor_copy(
                        ln["rbb"][:, co * 512:(co + 1) * 512],
                        ln["rbuf"][:, co * 512:(co + 1) * 512]), deps=[rcs_a])
                    war.r(("gt", r), rcs_last)
                    war.r(("x", slot), rcs_last)
                out_step = emit_ln(p, b, ln, war, rcs_last, p.const["eps_ln"],
                                   stage[tc % 2], war.w(("stg", tc % 2)))
                st = b.dma("sp", lambda s, bi=bi, tc=tc: [
                    (s.dma_start(xfm_out[c, :, bi * seq + tc * 512: bi * seq + tc * 512 + 512],
                                 stage[tc % 2][:, c * 512:(c + 1) * 512]), 16)
                    for c in range(NCH)], deps=[out_step], n=NCH)
                war.r(("stg", tc % 2), st)
                st_steps[(bi, tc)] = st
        b.step("sp", lambda s: s.nop(), deps=[st_steps[(nb - 1, ntc - 1)], cv_last])
        b.run()


def mk_ln_tiles(ctx, nc, tag, ext_psum=False, rbb=True, sq_dtype=BF16):
    sb = lambda n, s, d: ctx.enter_context(nc.sbuf_tensor(n, s, d))
    ps = {} if ext_psum else dict(
        pS1=ctx.enter_context(nc.psum_tensor(f"{tag}_pS1", [1, 512], F32)),
        pS2=ctx.enter_context(nc.psum_tensor(f"{tag}_pS2", [1, 512], F32)),
        pB1=ctx.enter_context(nc.psum_tensor(f"{tag}_pB1", [128, 512], F32)),
        pB2=ctx.enter_context(nc.psum_tensor(f"{tag}_pB2", [128, 512], F32)),
    )
    rb = dict(rbb=sb(f"{tag}_rbb", [128, NCH * 512], BF16)) if rbb else {}
    return dict(
        **ps, **rb,
        rbuf=sb(f"{tag}_rbuf", [128, NCH * 512], F32),
        sqb=[sb(f"{tag}_sqb{i}", [128, 512], sq_dtype) for i in range(2)],
        s1row=sb(f"{tag}_s1row", [1, 512], F32),
        s2row=sb(f"{tag}_s2row", [1, 512], F32),
        mb=sb(f"{tag}_mb", [128, 512], F32),
        s2m=sb(f"{tag}_s2m", [128, 512], F32),
        rstd=sb(f"{tag}_rstd", [128, 512], F32),
        scr=sb(f"{tag}_scr", [128, 512], F32),
    )


def emit_ln(p, b, ln, war, rdep, eps, stage, stage_war, use_rbb=True,
            out_inplace_f32=False):
    C = p.const
    pS1, pS2, pB1, pB2 = ln["pS1"], ln["pS2"], ln["pB1"], ln["pB2"]
    _kd = ln.get("_k", {})
    kmap = {id(pS1): _kd.get("pS1"), id(pS2): _kd.get("pS2"),
            id(pB1): _kd.get("pB1"), id(pB2): _kd.get("pB2")}
    k = lambda t: kmap.get(id(t)) or ("ps", id(t))
    ones = C["ones_col_b"] if use_rbb else C["ones_col_f"]
    ones_sq = C["ones_col_b"] if ln["sqb"][0].dtype == BF16 else ones
    KR = ln.get("_kr", 0)
    rbv = ln["rbuf"]
    srcof = (lambda c, _rb=ln.get("rbb"): _rb[:, c * 512:(c + 1) * 512]) if use_rbb \
        else (lambda c, _rb=rbv: _rb[:, c * 512:(c + 1) * 512])
    mms = None
    sqs = {}
    for c in range(NCH):
        sqs[c] = b.step("act", lambda a, c=c: a.activation(
            ln["sqb"][c % 2][:], srcof(c), AF.Square),
            deps=[rdep] + war.w(("lnsq", c % 2)))
        mms = b.step("pe", lambda t, c=c: [
            (t.matmul(pS1[:], ones[:], srcof(c),
                      start=(c == 0), stop=(c == NCH - 1), skip_group_check=True), 0),
            (t.matmul(pS2[:], ones_sq[:], ln["sqb"][c % 2][:],
                      start=(c == 0), stop=(c == NCH - 1), skip_group_check=True), 1)],
            deps=[sqs[c], rdep] + (war.w(k(pS1)) + war.w(k(pS2)) if c == 0 else []))
        war.r(("lnsq", c % 2), mms)
        war.r(("rb", KR, c), sqs[c])
        war.r(("rb", KR, c), mms)
    rows = b.step("dve", lambda v: [
        (v.tensor_copy(ln["s1row"][:], pS1[:]), 0),
        (v.tensor_copy(ln["s2row"][:], pS2[:]), 1)],
        deps=[mms] + war.w(("lnrow",)))
    war.r(k(pS1), rows)
    war.r(k(pS2), rows)
    bc = b.step("pe", lambda t: [
        (t.matmul(pB1[:], C["ones_row_f"][:], ln["s1row"][:], start=True, stop=True), 0),
        (t.matmul(pB2[:], C["ones_row_f"][:], ln["s2row"][:], start=True, stop=True), 1)],
        deps=[rows] + war.w(k(pB1)) + war.w(k(pB2)))
    war.r(("lnrow",), bc)
    stats = b.step("act", lambda a: [
        (a.activation(ln["mb"][:], pB1[:], AF.Copy, scale=1.0 / D), 0),
        (a.activation(ln["s2m"][:], pB2[:], AF.Copy, scale=1.0 / D), 1)],
        deps=[bc] + war.w(("lnstat",)))
    war.r(k(pB1), stats)
    war.r(k(pB2), stats)
    msq = b.step("act", lambda a: a.activation(ln["scr"][:], ln["mb"][:], AF.Square),
                 deps=[stats])
    var = b.step("dve", lambda v: v.tensor_tensor(ln["s2m"][:], ln["s2m"][:], ln["scr"][:],
                                                  ALU.subtract), deps=[msq])
    rs1 = b.step("act", lambda a, eps=eps: a.activation(ln["rstd"][:], ln["s2m"][:],
                                                        AF.Ln, bias=eps[:]), deps=[var])
    rst = b.step("act", lambda a: a.activation(ln["rstd"][:], ln["rstd"][:], AF.Exp,
                                               scale=-0.5), deps=[rs1])
    out = None
    osub = None
    for c in range(NCH):
        osub = b.step("dve", lambda v, c=c, rbv=rbv: v.tensor_tensor(
            ln["scr"][:], rbv[:, c * 512:(c + 1) * 512], ln["mb"][:], ALU.subtract),
            deps=[rst, osub] + (stage_war if c == 0 else []))
        if out_inplace_f32:
            out = b.step("dve", lambda v, c=c, rbv=rbv: v.tensor_tensor(
                rbv[:, c * 512:(c + 1) * 512], ln["scr"][:], ln["rstd"][:],
                ALU.mult), deps=[osub])
        else:
            out = b.step("dve", lambda v, c=c: v.tensor_tensor(
                stage[:, c * 512:(c + 1) * 512], ln["scr"][:], ln["rstd"][:], ALU.mult),
                deps=[osub])
        war.r(("rb", KR, c), osub)
        war.r(("rb", KR, c), out)
        ln.setdefault("_outs", []).append(out)
    war.r(("lnstat",), out)
    return out


def emit_convert_steps(p, b, war, bufs, jobs):
    for src, dst, mode, gain in jobs:
        R, Cc = src.shape
        nband = R // 128
        for hc in range(nband):
            slot = hc % 4
            ld = b.dma("pool", lambda g, src=src, hc=hc, slot=slot, Cc=Cc:
                       g.dma_start(bufs[slot][:, 0:Cc], src[hc * 128:(hc + 1) * 128, :]),
                       deps=war.w(("b", slot)))
            dep = ld
            if gain is not None:
                dep = b.step("dve", lambda v, slot=slot, gain=gain, hc=hc, Cc=Cc:
                             v.tensor_scalar(bufs[slot][:, 0:Cc], bufs[slot][:, 0:Cc],
                                             gain[:, hc:hc + 1], None, ALU.mult),
                             deps=[ld])
            if mode == "copy":
                st = b.dma("sp", lambda s, dst=dst, hc=hc, slot=slot, Cc=Cc:
                           s.dma_start(dst[hc * 128:(hc + 1) * 128, :], bufs[slot][:, 0:Cc]),
                           deps=[dep])
                war.r(("b", slot), st)
            else:
                def fn(s, dst=dst, hc=hc, slot=slot, Cc=Cc):
                    return [(s.dma_start(dst[cb, :, hc * 128:(hc + 1) * 128],
                                         bufs[slot][:, cb * 128:(cb + 1) * 128]), 16)
                            for cb in range(Cc // 128)]
                st = b.dma("sp", fn, deps=[dep], n=Cc // 128)
                war.r(("b", slot), st)
    return st


def emit_convert(p, jobs):
    nc = p.nc
    with contextlib.ExitStack() as ctx:
        bufs = [ctx.enter_context(nc.sbuf_tensor(f"cv_b{i}", [128, 4096], BF16))
                for i in range(4)]
        b = p.block("convert")
        st = emit_convert_steps(p, b, War(), bufs, jobs)
        b.step("sp", lambda s: s.nop(), deps=[st])
        b.run()


def emit_router_block(p, b, war, mode, blk, x_ap_of_dc, x_dep, wr_sb, g8big,
                      tiles, ps, extra_deps=()):
    gates, tmp = tiles["gates"], tiles["tmp"]
    plog = ps["log"]
    red4 = tiles["red4"]
    _pk = ps.get("keys", {})
    kk = lambda t: _pk.get("log") if t is plog else (
        _pk.get("tr") if t is ps["tr"] else ("ps", id(t)))
    mml = b.step("pe", lambda t: [
        (t.matmul(plog[:], x_ap_of_dc(dc), wr_sb[:, dc * 8:(dc + 1) * 8],
                  start=(dc == 0), stop=(dc == NCH - 1)),
         1 if dc == NCH - 1 else 0)
        for dc in range(NCH)], deps=[x_dep] + list(extra_deps) + war.w(kk(plog)))
    if mode == "softmax":
        nmx = b.step("dve", lambda v: v.tensor_reduce(
            red4[:, 0:1], plog[:], axis=AX, op=ALU.max, negate=True),
            deps=[mml] + war.w(("red4",)))
        exs = b.step("act", lambda a: a.activation(gates[:, 0:8], plog[:], AF.Exp,
                                                   bias=red4[:, 0:1]),
                     deps=[nmx] + war.w(("gates",)))
        war.r(kk(plog), exs)
        den = b.step("dve", lambda v: v.reduce_sum(red4[:, 1:2], gates[:, 0:8], axis=AX),
                     deps=[exs])
        rc1 = b.step("act", lambda a: a.activation(red4[:, 2:3], red4[:, 1:2], AF.Ln),
                     deps=[den])
        rc2 = b.step("act", lambda a: a.activation(red4[:, 2:3], red4[:, 2:3], AF.Exp,
                                                   scale=-1.0), deps=[rc1])
        prb = b.step("dve", lambda v: v.tensor_scalar(
            gates[:, 0:8], gates[:, 0:8], red4[:, 2:3], None, ALU.mult), deps=[rc2])
    else:
        exs = b.step("act", lambda a: a.activation(gates[:, 0:8], plog[:], AF.Sigmoid),
                     deps=[mml] + war.w(("gates",)))
        war.r(kk(plog), exs)
        den = b.step("dve", lambda v: v.reduce_sum(red4[:, 1:2], gates[:, 0:8], axis=AX),
                     deps=[exs])
        rc1 = b.step("act", lambda a: a.activation(red4[:, 2:3], red4[:, 1:2], AF.Ln),
                     deps=[den])
        rc2 = b.step("act", lambda a: a.activation(red4[:, 2:3], red4[:, 2:3], AF.Exp,
                                                   scale=-1.0), deps=[rc1])
        prb = rc2
    t1_ = b.step("dve", lambda v: v.tensor_reduce(red4[:, 0:1], gates[:, 0:8], axis=AX,
                                                  op=ALU.max), deps=[prb])
    t2_ = b.step("dve", lambda v: v.tensor_scalar(tmp[:, 0:8], gates[:, 0:8],
                                                  red4[:, 0:1], None, ALU.is_ge),
                 deps=[t1_] + war.w(("tmp",)))
    t3_ = b.step("dve", lambda v: v.scalar_tensor_tensor(
        tmp[:, 0:8], tmp[:, 0:8], -2.0, gates[:, 0:8], ALU.mult, ALU.add), deps=[t2_])
    t4_ = b.step("dve", lambda v: v.tensor_reduce(red4[:, 3:4], tmp[:, 0:8], axis=AX,
                                                  op=ALU.max), deps=[t3_])
    top = b.step("dve", lambda v: v.tensor_scalar(tmp[:, 0:8], gates[:, 0:8],
                                                  red4[:, 3:4], None, ALU.is_ge), deps=[t4_])
    if mode == "softmax":
        fin = b.step("dve", lambda v: v.tensor_tensor(
            gates[:, 0:8], gates[:, 0:8], tmp[:, 0:8], ALU.mult), deps=[top])
    else:
        f1 = b.step("dve", lambda v: v.tensor_tensor(gates[:, 0:8], gates[:, 0:8],
                                                     tmp[:, 0:8], ALU.mult), deps=[top])
        fin = b.step("dve", lambda v: v.tensor_scalar(gates[:, 0:8], gates[:, 0:8],
                                                      red4[:, 2:3], None, ALU.mult),
                     deps=[f1])
    war.r(("red4",), fin)
    war.r(("tmp",), fin)
    tr = b.step("pe", lambda t: t.transpose(ps["tr"][:], gates[:],
                                            p.const["ident_f"][:]),
                deps=[fin] + war.w(kk(ps["tr"])))
    war.r(("gates",), tr)
    ev = b.step("dve", lambda v: v.tensor_copy(g8big[0:8, :], ps["tr"][0:8, :]),
                deps=[tr] + war.w(("g8st", id(g8big))))
    war.r(kk(ps["tr"]), ev)
    return ev, mml


def emit_l0_ffn(p, xfm_in, xfm_out, wi_scr, wo_scr, wr_ap, T, tag="l0f"):
    nc = p.nc
    ngrp = T // 512
    g8_dram = nc.dram_tensor(tag + "_g8d", [8, T], BF16, kind="Internal").ap()
    with contextlib.ExitStack() as ctx:
        sb = lambda n, s, d: ctx.enter_context(nc.sbuf_tensor(n, s, d))
        psum = lambda n, s, d=F32: ctx.enter_context(nc.psum_tensor(n, s, d))
        xbuf = [sb(f"l0f_x{i}", [128, NCH * 512], BF16) for i in range(2)]
        wrsb = sb("l0f_wr", [128, NCH * 8], BF16)
        g8st = [sb(f"l0f_g8st{i}", [8, 128], BF16) for i in range(2)]
        grow = [sb(f"l0f_grow{i}", [1, 512], BF16) for i in range(2)]
        gb = [sb(f"l0f_gb{i}", [128, 512], BF16) for i in range(2)]
        wibuf = [sb(f"l0f_wi{i}", [128, NCH * 1024], BF16) for i in range(2)]
        wobuf = [sb(f"l0f_wo{i}", [128, 8 * 128], BF16) for i in range(16)]
        heall = [sb(f"l0f_he{e}", [128, NCH * 512], BF16) for e in range(8)]
        rtiles = dict(gates=sb("l0f_gts", [128, 128], F32),
                      tmp=sb("l0f_tmp", [128, 8], F32),
                      red4=sb("l0f_red4", [128, 4], F32))
        ln = mk_ln_tiles(ctx, nc, "l0f")
        stage = [sb(f"l0f_stg{i}", [128, NCH * 512], BF16) for i in range(2)]

        b = p.block("l0f_router")
        war = War()
        b.dma("pool", lambda g: [
            (g.dma_start(wrsb[:, dc * 8:(dc + 1) * 8], wr_ap[dc * 128:(dc + 1) * 128, :]), 16)
            for dc in range(NCH)], n=NCH)
        wld = b.steps[-1]
        zg = b.step("pool", lambda g: g.memset(rtiles["gates"][:], 0.0))
        with nc.psum_tensor("l0f_plog", [128, 8], F32) as plog, \
                nc.psum_tensor("l0f_ptr", [128, 128], F32) as ptr:
            ps = {"log": plog, "tr": ptr}
            for gi in range(ngrp):
                slot = gi % 2
                ld = b.dma("sp", lambda s, gi=gi, slot=slot: [
                    (s.dma_start(xbuf[slot][:, dc * 512:(dc + 1) * 512],
                                 xfm_in[dc, :, gi * 512:(gi + 1) * 512]), 16)
                    for dc in range(NCH)], deps=war.w(("x", slot)), n=NCH)
                for tb in range(4):
                    blk = gi * 4 + tb
                    x_of = (lambda slot, tb: lambda dc:
                            xbuf[slot][:, dc * 512 + tb * 128: dc * 512 + tb * 128 + 128]
                            )(slot, tb)
                    ev, mml = emit_router_block(p, b, war, "softmax", blk, x_of,
                                                ld, wrsb, g8st[blk % 2], rtiles, ps,
                                                extra_deps=[wld, zg])
                    war.r(("x", slot), mml)
                    g8w = b.dma("sp", lambda s, blk=blk: s.dma_start(
                        g8_dram[0:8, blk * 128:(blk + 1) * 128], g8st[blk % 2][:]),
                        deps=[ev])
                    war.r(("g8st", id(g8st[blk % 2])), g8w)
            b.step("sp", lambda s: s.nop(), deps=[g8w])
            b.run()

        b = p.block("l0f_main")
        war = War()
        phe = [psum(f"l0f_phe{i}", [128, 512]) for i in range(2)]
        pout = [psum(f"l0f_pout{i}", [128, 512]) for i in range(2)]
        st_last = None
        for gi in range(ngrp):
            slot = gi % 2
            ld = b.dma("sp", lambda s, gi=gi, slot=slot: [
                (s.dma_start(xbuf[slot][:, dc * 512:(dc + 1) * 512],
                             xfm_in[dc, :, gi * 512:(gi + 1) * 512]), 16)
                for dc in range(NCH)], deps=war.w(("x", slot)), n=NCH)
            for e in range(8):
                wslot = e % 2
                rw = b.dma("sp", lambda s, gi=gi, e=e: s.dma_start(
                    grow[e % 2][:], g8_dram[e:e + 1, gi * 512:(gi + 1) * 512]),
                    deps=war.w(("grow", e % 2)))
                wl = b.dma("pool", lambda g, e=e, wslot=wslot: [
                    (g.dma_start(wibuf[wslot][:, dc * 1024:(dc + 1) * 1024],
                                 wi_scr[e, dc * 128:(dc + 1) * 128, :]), 16)
                    for dc in range(NCH)], deps=war.w(("wi", wslot)), n=NCH)
                gbs = e % 2
                bcm = b.step("pe", lambda t, e=e, gbs=gbs: t.matmul(
                    pout[gbs][:], p.const["ones_row_b"][:],
                    grow[e % 2][:], start=True, stop=True),
                    deps=[rw] + war.w(("pout", gbs)))
                war.r(("grow", e % 2), bcm)
                gbe = b.step("act", lambda a, gbs=gbs: a.activation(
                    gb[gbs][:], pout[gbs][:], AF.Identity),
                    deps=[bcm] + war.w(("gb", gbs)))
                war.r(("pout", gbs), gbe)
                for hc in range(NCH):
                    j = e * NCH + hc
                    r = j % 2
                    mmh = b.step("pe", lambda t, hc=hc, slot=slot, wslot=wslot, r=r: [
                        (t.matmul(phe[r][:],
                                  wibuf[wslot][:, dc * 1024 + hc * 128: dc * 1024 + hc * 128 + 128],
                                  xbuf[slot][:, dc * 512:(dc + 1) * 512],
                                  start=(dc == 0), stop=(dc == NCH - 1)),
                         1 if dc == NCH - 1 else 0)
                        for dc in range(NCH)],
                        deps=[ld, wl] + war.w(("phe", r)))
                    war.r(("x", slot), mmh)
                    war.r(("wi", wslot), mmh)
                    gev = b.step("dve", lambda v, e=e, hc=hc, r=r, gbs=gbs: v.tensor_tensor(
                        heall[e][:, hc * 512:(hc + 1) * 512], phe[r][:], gb[gbs][:],
                        ALU.mult), deps=[gbe, mmh] + war.w(("he", e, hc)))
                    war.r(("phe", r), gev)
                    war.r(("gb", gbs), gev)
            hev = gev
            for co in range(NCH):
                r = co % 2
                wo_ld = {}
                for e in range(8):
                    ws = (co % 2) * 8 + e
                    wo_ld[e] = b.dma("pool", lambda g, co=co, e=e, ws=ws:
                                     g.dma_start(wobuf[ws][:], wo_scr[e, co, :, :]),
                                     deps=war.w(("wo", ws)))
                def mmo_fn(t, co=co, r=r):
                    out = []
                    for e in range(8):
                        ws = (co % 2) * 8 + e
                        for hc in range(NCH):
                            out.append((t.matmul(
                                pout[r][:], wobuf[ws][:, hc * 128:(hc + 1) * 128],
                                heall[e][:, hc * 512:(hc + 1) * 512],
                                start=(e == 0 and hc == 0), stop=(e == 7 and hc == NCH - 1)),
                                1 if (e == 7 and hc == NCH - 1) else 0))
                    return out
                mmo = b.step("pe", mmo_fn,
                             deps=[hev] + list(wo_ld.values()) + war.w(("pout", r)))
                for e in range(8):
                    war.r(("wo", (co % 2) * 8 + e), mmo)
                    for hc in range(NCH):
                        war.r(("he", e, hc), mmo)
                rcs_a = b.step("dve", lambda v, co=co, r=r, slot=slot: v.tensor_tensor(
                    ln["rbuf"][:, co * 512:(co + 1) * 512], pout[r][:],
                    xbuf[slot][:, co * 512:(co + 1) * 512], ALU.add),
                    deps=[mmo] + war.w(("rb", co)))
                rcs = b.step("dve", lambda v, co=co: v.tensor_copy(
                    ln["rbb"][:, co * 512:(co + 1) * 512],
                    ln["rbuf"][:, co * 512:(co + 1) * 512]), deps=[rcs_a])
                war.r(("pout", r), rcs)
                war.r(("x", slot), rcs)
            out_step = emit_ln(p, b, ln, war, rcs, p.const["eps_ln"],
                               stage[gi % 2], war.w(("stg", gi % 2)))
            st_last = b.dma("sp", lambda s, gi=gi: [
                (s.dma_start(xfm_out[c, :, gi * 512:(gi + 1) * 512],
                             stage[gi % 2][:, c * 512:(c + 1) * 512]), 16)
                for c in range(NCH)], deps=[out_step], n=NCH)
            war.r(("stg", gi % 2), st_last)
        b.step("sp", lambda s: s.nop(), deps=[st_last])
        b.run()


def emit_l1_attn(p, xfm_in, xfm_out, W, nb, seq):
    nc = p.nc
    C = p.const
    nqc = seq // 512
    with contextlib.ExitStack() as ctx:
        sb = lambda n, s, d: ctx.enter_context(nc.sbuf_tensor(n, s, d))
        psum = lambda n, s, d=F32: ctx.enter_context(nc.psum_tensor(n, s, d))
        wbuf = [sb(f"l1a_w{i}", [128, NCH * 1024], BF16) for i in range(2)]
        x2buf = [sb(f"l1a_x{i}", [128, NCH * 512], BF16) for i in range(2)]
        kfm = sb("l1a_kfm", [128, NCH * seq], BF16)
        vtok = sb("l1a_vtok", [128, (seq // 128) * 1040], BF16)
        qbuf = [sb("l1a_q0", [128, NCH * 512], BF16)] * 2
        eslab = sb("l1a_eslab", [128, (seq // 128) * 512], BF16)
        obuf = [sb("l1a_o0", [128, NCH * 512], BF16)] * 2
        wrow = [sb(f"l1a_wr{i}", [1, 512], F32) for i in range(2)]
        wrowb = [sb(f"l1a_wrb{i}", [1, 512], BF16) for i in range(2)]
        wb = [sb(f"l1a_wb{i}", [128, 512], F32) for i in range(2)]
        ln = mk_ln_tiles(ctx, nc, "l1a", ext_psum=True)
        stage = [sb("l1a_stg0", [128, NCH * 512], BF16)] * 2
        pb = [psum(f"l1a_pb{i}", [128, 512]) for i in range(6)]
        pr = [psum(f"l1a_pr{i}", [1, 512]) for i in range(2)]
        ln.update(pS1=pb[0][0:1, :], pS2=pb[1][0:1, :], pB1=pb[0][:], pB2=pb[1][:])
        bk = lambda t: ("ps", id(t))
        ln["_k"] = dict(pS1=bk(pb[0]), pS2=bk(pb[1]), pB1=bk(pb[0]), pB2=bk(pb[1]))

        b = p.block("l1a_main")
        war = War()
        st_last = None
        vz = b.step("pool", lambda g: g.memset(vtok[:], 1.0))
        for bi in range(nb):
            wkv = []
            for i, k in enumerate(["Wk", "Wv"]):
                wkv.append(b.dma("pool", lambda g, k=k, i=i: [
                    (g.dma_start(wbuf[i][:, dc * 1024:(dc + 1) * 1024],
                                 W[k][dc * 128:(dc + 1) * 128, :]), 16)
                    for dc in range(NCH)], deps=war.w(("wb", i)), n=NCH))
            for tc in range(seq // 512):
                slot = tc % 2
                ld = b.dma("sp", lambda s, bi=bi, tc=tc, slot=slot: [
                    (s.dma_start(x2buf[slot][:, dc * 512:(dc + 1) * 512],
                                 xfm_in[dc, :, bi * seq + tc * 512: bi * seq + tc * 512 + 512]), 16)
                    for dc in range(NCH)], deps=war.w(("x", slot)), n=NCH)
                for c in range(NCH):
                    j = tc * NCH + c
                    r = j % 2
                    mmk = b.step("pe", lambda t, c=c, slot=slot, r=r: [
                        (t.matmul(pb[2 + r][:],
                                  wbuf[0][:, dc * 1024 + c * 128: dc * 1024 + c * 128 + 128],
                                  x2buf[slot][:, dc * 512:(dc + 1) * 512],
                                  start=(dc == 0), stop=(dc == NCH - 1)),
                         1 if dc == NCH - 1 else 0)
                        for dc in range(NCH)],
                        deps=[ld, wkv[0]] + war.w(("pst", r)))
                    war.r(("x", slot), mmk)
                    war.r(("wb", 0), mmk)
                    evk = b.step("act", lambda a, c=c, tc=tc, r=r: a.activation(
                        kfm[:, c * seq + tc * 512: c * seq + tc * 512 + 512],
                        pb[2 + r][:], AF.Identity), deps=[mmk] + war.w(("kfm", c, tc)))
                    war.r(("pst", r), evk)
                for tb4 in range(4):
                    tb = tc * 4 + tb4
                    for half in range(2):
                        j2 = tb * 2 + half
                        r = j2 % 2
                        mmv = b.step("pe", lambda t, tb=tb, tb4=tb4, half=half, slot=slot, r=r: [
                            (t.matmul(pb[2 + r][:],
                                      x2buf[slot][:, dc * 512 + tb4 * 128: dc * 512 + tb4 * 128 + 128],
                                      wbuf[1][:, dc * 1024 + half * 512: dc * 1024 + half * 512 + 512],
                                      start=(dc == 0), stop=(dc == NCH - 1)),
                             1 if dc == NCH - 1 else 0)
                            for dc in range(NCH)],
                            deps=[ld, wkv[1]] + war.w(("pst", r)))
                        war.r(("x", slot), mmv)
                        war.r(("wb", 1), mmv)
                        evv = b.step("act", lambda a, tb=tb, half=half, r=r: a.activation(
                            vtok[:, tb * 1040 + half * 8 * 65:
                                 tb * 1040 + half * 8 * 65 + 520].rearrange(
                                "p (h d) -> p h d", d=65)[:, :, 0:64],
                            pb[2 + r][:].rearrange("p (h d) -> p h d", d=64),
                            AF.Identity),
                            deps=[mmv, vz] + war.w(("vtok", tb, half)))
                        war.r(("pst", r), evv)
            kv_done = evv
            wqo = []
            for i, k in enumerate(["Wq", "Wo"]):
                wqo.append(b.dma("pool", lambda g, k=k, i=i: [
                    (g.dma_start(wbuf[i][:, dc * 1024:(dc + 1) * 1024],
                                 W[k][dc * 128:(dc + 1) * 128, :]), 16)
                    for dc in range(NCH)], deps=war.w(("wb", i)), n=NCH))
            for qc in range(nqc):
                slot = qc % 2
                ld = b.dma("sp", lambda s, bi=bi, qc=qc, slot=slot: [
                    (s.dma_start(x2buf[slot][:, dc * 512:(dc + 1) * 512],
                                 xfm_in[dc, :, bi * seq + qc * 512: bi * seq + qc * 512 + 512]), 16)
                    for dc in range(NCH)], deps=war.w(("x", slot)), n=NCH)
                for c in range(NCH):
                    j = qc * NCH + c
                    r = j % 2
                    mmq = b.step("pe", lambda t, c=c, slot=slot, r=r: [
                        (t.matmul(pb[2 + r][:],
                                  wbuf[0][:, dc * 1024 + c * 128: dc * 1024 + c * 128 + 128],
                                  x2buf[slot][:, dc * 512:(dc + 1) * 512],
                                  start=(dc == 0), stop=(dc == NCH - 1)),
                         1 if dc == NCH - 1 else 0)
                        for dc in range(NCH)],
                        deps=[ld, wqo[0], kv_done] + war.w(("pst", r)))
                    war.r(("x", slot), mmq)
                    war.r(("wb", 0), mmq)
                    evq = b.step("act", lambda a, c=c, r=r, slot=slot: a.activation(
                        qbuf[slot][:, c * 512:(c + 1) * 512], pb[2 + r][:], AF.Identity,
                        scale=0.125), deps=[mmq] + war.w(("qb", c)))
                    war.r(("pst", r), evq)
                qdone = evq
                def normalizer(h, av_last, cs_last):
                    cc, off, hp = h // 2, (h % 2) * 64, h % 2
                    rc1 = b.step("act", lambda a, hp=hp: a.activation(
                        wrow[hp][:], pb[4 + hp][64:65, :], AF.Ln),
                        deps=[av_last] + war.w(("wrow", hp)))
                    rc2 = b.step("act", lambda a, hp=hp: a.activation(
                        wrowb[hp][:], wrow[hp][:], AF.Exp, scale=-1.0), deps=[rc1])
                    bcw = b.step("pe", lambda t, hp=hp: t.matmul(
                        pb[0][:], C["ones_row_b"][:], wrowb[hp][:], start=True, stop=True),
                        deps=[rc2] + war.w(bk(pb[0])) + war.w(("wrowb", hp)))
                    war.r(("wrow", hp), bcw)
                    war.r(("wrowb", hp), bcw)
                    evw = b.step("act", lambda a, hp=hp: a.activation(
                        wb[hp][:], pb[0][:], AF.Identity), deps=[bcw] + war.w(("wbt", hp)))
                    war.r(bk(pb[0]), evw)
                    osc = b.step("dve", lambda v, cc=cc, off=off, hp=hp, slot=slot:
                                 v.tensor_tensor(
                        obuf[slot][off:off + 64, cc * 512:(cc + 1) * 512],
                        pb[4 + hp][0:64, :], wb[hp][0:64, :], ALU.mult),
                        deps=[evw, av_last] + war.w(("ob", h)))
                    war.r(("pb4", hp), osc)
                    war.r(("wbt", hp), osc)
                    return osc

                prev_head = None
                osc = None
                for h in range(16):
                    cc, off, hp = h // 2, (h % 2) * 64, h % 2
                    nkb = 4 * qc + 4
                    pend = []
                    cs = av = None
                    for kb in range(nkb + 1):
                        if kb < nkb:
                            qoff = max(0, (kb - 4 * qc) * 128)
                            r = kb % 2
                            mmst = b.step("pe", lambda t, cc=cc, off=off, kb=kb,
                                          qoff=qoff, slot=slot, r=r: t.matmul(
                                pb[2 + r][0:128, qoff:512],
                                kfm[off:off + 64, cc * seq + kb * 128: cc * seq + kb * 128 + 128],
                                qbuf[slot][off:off + 64, cc * 512 + qoff: cc * 512 + 512],
                                start=True, stop=True),
                                deps=[qdone, kv_done] + war.w(("pst", r)))
                            war.r(("qb", cc), mmst)
                            eve = b.step("act", lambda a, kb=kb, qoff=qoff, r=r:
                                         a.activation(
                                eslab[:, kb * 512 + qoff: kb * 512 + 512],
                                pb[2 + r][0:128, qoff:512], AF.Exp),
                                deps=[mmst] + war.w(("esl", kb)))
                            war.r(("pst", r), eve)
                            pend.append((kb, qoff, eve))
                        while pend and (kb == nkb or len(pend) > 2):
                            kb2, qoff2, eve2 = pend.pop(0)
                            av = b.step("pe", lambda t, kb=kb2, qoff=qoff2, h=h,
                                        nkb=nkb, hp=hp: t.matmul(
                                pb[4 + hp][0:65, qoff:512],
                                vtok[:, kb * 1040 + h * 65: kb * 1040 + h * 65 + 65],
                                eslab[:, kb * 512 + qoff: kb * 512 + 512],
                                start=(kb == 0), stop=(kb == nkb - 1),
                                skip_group_check=True),
                                deps=[eve2] + (war.w(("pb4", hp)) if kb2 == 0 else []))
                            war.r(("esl", kb2), av)
                            war.r(("vtok", kb2, 0), av)
                            war.r(("vtok", kb2, 1), av)
                    assert not pend
                    if prev_head is not None:
                        osc = normalizer(*prev_head)
                    prev_head = (h, av, None)
                osc = normalizer(*prev_head)
                odone = osc

                rcs = None
                for co in range(NCH):
                    j = qc * NCH + co
                    r = j % 2
                    mmo = b.step("pe", lambda t, co=co, slot=slot, r=r: [
                        (t.matmul(pb[2 + r][:],
                                  wbuf[1][:, cc2 * 1024 + co * 128: cc2 * 1024 + co * 128 + 128],
                                  obuf[slot][:, cc2 * 512:(cc2 + 1) * 512],
                                  start=(cc2 == 0), stop=(cc2 == NCH - 1)),
                         1 if cc2 == NCH - 1 else 0)
                        for cc2 in range(NCH)],
                        deps=[odone, wqo[1]] + war.w(("pst", r)))
                    war.r(("wb", 1), mmo)
                    for h2 in range(16):
                        war.r(("ob", h2), mmo)
                    rcs_a = b.step("dve", lambda v, co=co, r=r, slot=slot: v.tensor_tensor(
                        ln["rbuf"][:, co * 512:(co + 1) * 512], pb[2 + r][:],
                        x2buf[slot][:, co * 512:(co + 1) * 512], ALU.add),
                        deps=[mmo] + war.w(("rb", co)))
                    war.r(("pst", r), rcs_a)
                    war.r(("x", slot), rcs_a)
                    rcs = b.step("dve", lambda v, co=co: v.tensor_copy(
                        ln["rbb"][:, co * 512:(co + 1) * 512],
                        ln["rbuf"][:, co * 512:(co + 1) * 512]), deps=[rcs_a])
                out_step = emit_ln(p, b, ln, war, rcs, p.const["eps_ln"],
                                   stage[0], war.w(("stg",)))
                st_last = b.dma("sp", lambda s, bi=bi, qc=qc: [
                    (s.dma_start(xfm_out[c, :, bi * seq + qc * 512: bi * seq + qc * 512 + 512],
                                 stage[0][:, c * 512:(c + 1) * 512]), 16)
                    for c in range(NCH)], deps=[out_step], n=NCH)
                war.r(("stg",), st_last)
        b.step("sp", lambda s: s.nop(), deps=[st_last])
        b.run()


def emit_l1_ffn(p, xfm_in, out_ap, w1_scr, w2_scr, b1_list, b2r_ap, b2s_ap,
                cent_ap, nb, seq):
    nc = p.nc
    C = p.const
    T = nb * seq
    ngrp = T // 1024
    NE = 10
    EXPERTS = [8, 9] + list(range(8))
    g8_dram = nc.dram_tensor("l1f_g8d", [8, T], BF16, kind="Internal").ap()
    with contextlib.ExitStack() as ctx:
        sb = lambda n, s, d: ctx.enter_context(nc.sbuf_tensor(n, s, d))
        wb = sb("l1f_wb", [128, 16384], BF16)
        bigA = sb("l1f_bigA", [128, 32768], BF16)
        ubuf = [sb("l1f_u0", [128, 8192], BF16)] * 2
        bigB = sb("l1f_bigB", [128, 8192], BF16)
        acc = sb("l1f_acc", [128, NCH * 1024], F32)
        ln = mk_ln_tiles(ctx, nc, "l1f", ext_psum=True, rbb=False, sq_dtype=BF16)
        rstdb = [sb(f"l1f_rstdb{i}", [128, 512], F32) for i in range(2)]
        gbt = sb("l1f_gbt", [128, 1024], BF16)
        spt = sb("l1f_spt", [128, 512], BF16)
        grow = [sb("l1f_grow0", [1, 1024], BF16)] * 2
        wrow10 = sb("l1f_wrow10", [16, 1024], BF16)
        b1sb = sb("l1f_b1", [128, NE * 32], F32)
        b2sb = sb("l1f_b2", [16, 1024], BF16)
        centsb = sb("l1f_cent", [128, NCH * 8], BF16)
        g8st = [sb(f"l1f_g8st{i}", [8, 128], BF16) for i in range(2)]
        rtiles = dict(gates=sb("l1f_gts", [128, 128], F32),
                      tmp=sb("l1f_tmp", [128, 8], F32),
                      red4=sb("l1f_red4", [128, 4], F32))
        outst = [bigA.bitcast(F32)[:, 12288:13312]] * 2
        uslice = bigA[:, 28672:32768]
        rbuf1 = bigA.bitcast(F32)[:, 8192:12288]
        rbuf0 = None
        pool8 = [ctx.enter_context(nc.psum_tensor(f"l1f_pp{i}", [128, 512], F32))
                 for i in range(8)]
        ln.update(pS1=pool8[4][0:1, :], pS2=pool8[5][0:1, :],
                  pB1=pool8[6][:], pB2=pool8[7][:])
        bk = lambda t: ("ps", id(t))
        kS1, kS2, kB1, kB2 = bk(pool8[4]), bk(pool8[5]), bk(pool8[6]), bk(pool8[7])
        ln["_k"] = dict(pS1=kS1, pS2=kS2, pB1=kB1, pB2=kB2)

        u_of = lambda up, dc: ubuf[up][:, dc * 1024:(dc + 1) * 1024]
        xn_of = lambda dc: bigB[:, dc * 1024:(dc + 1) * 1024]
        h_of = lambda hc: bigA[:, hc * 1024:(hc + 1) * 1024]
        w1t = lambda dc, hcin: wb[:, dc * 2048 + hcin * 128: dc * 2048 + hcin * 128 + 128]
        w2t = lambda hcin, c: wb[:, hcin * 1024 + c * 128: hcin * 1024 + c * 128 + 128]

        b = p.block("l1f_pre")
        b.dma("pool", lambda g: [
            (g.dma_start(centsb[:, dc * 8:(dc + 1) * 8],
                         cent_ap[dc * 128:(dc + 1) * 128, :]), 16)
            for dc in range(NCH)], n=NCH)
        for e in range(NE):
            load_bias_cols(b, b1sb[:, e * 32:(e + 1) * 32], b1_list[e], 32)
        b.dma("pool", lambda g: g.dma_start(b2sb[0:8, :], b2r_ap[:, :]))
        b.dma("pool", lambda g: g.dma_start(b2sb[8:10, :], b2s_ap[:, :]))
        b.step("pool", lambda g: g.memset(wrow10[:], 1.0))
        b.step("pool", lambda g: g.memset(rtiles["gates"][:], 0.0))
        b.run()

        b = p.block("l1f_main")
        war = War()
        st_last = None

        def prep_load(gi):
            up = 0
            return b.dma("sp", lambda s, gi=gi, up=up: [
                (s.dma_start(u_of(up, dc), xfm_in[dc, :, gi * 1024:(gi + 1) * 1024]), 16)
                for dc in range(NCH)], deps=war.w(("u", up)), n=NCH)

        def prep(gi, uld):
            up = 0
            g8w = None
            for tb in range(8):
                x_of = (lambda tb, up: lambda dc:
                        u_of(up, dc)[:, tb * 128:(tb + 1) * 128])(tb, up)
                ev, mml = emit_router_block(
                    p, b, war, "sigmoid", tb, x_of, uld, centsb, g8st[tb % 2],
                    rtiles, {"log": pool8[0][:, 0:8], "tr": pool8[6][:, 0:128],
                             "keys": {"log": bk(pool8[0]), "tr": kB1}})
                war.r(("u", up), mml)
                g8w = b.dma("sp", lambda s, gi=gi, tb=tb: s.dma_start(
                    g8_dram[0:8, gi * 1024 + tb * 128: gi * 1024 + tb * 128 + 128],
                    g8st[tb % 2][:]), deps=[ev])
                war.r(("g8st", id(g8st[tb % 2])), g8w)
            evrs = []
            for tc in range(2):
                mms = None
                for dc in range(NCH):
                    sq = b.step("act", lambda a, tc=tc, dc=dc, up=up: a.activation(
                        ln["sqb"][dc % 2][:], u_of(up, dc)[:, tc * 512:(tc + 1) * 512],
                        AF.Square), deps=[uld] + war.w(("lnsq", dc % 2)))
                    war.r(("u", up), sq)
                    mms = b.step("pe", lambda t, dc=dc: t.matmul(
                        ln["pS1"], C["ones_col_b"][:], ln["sqb"][dc % 2][:],
                        start=(dc == 0), stop=(dc == NCH - 1), skip_group_check=True),
                        deps=[sq] + (war.w(kS1) if dc == 0 else []))
                    war.r(("lnsq", dc % 2), mms)
                r1 = b.step("act", lambda a: a.activation(
                    ln["s1row"][:], ln["pS1"], AF.Ln, scale=1.0 / D,
                    bias=C["eps_rms"][0:1, :]), deps=[mms] + war.w(("s1row",)))
                war.r(kS1, r1)
                r2 = b.step("act", lambda a: a.activation(
                    ln["s1row"][:], ln["s1row"][:], AF.Exp, scale=-0.5), deps=[r1])
                bcr = b.step("pe", lambda t: t.matmul(
                    pool8[7][:], C["ones_row_f"][:], ln["s1row"][:], start=True,
                    stop=True), deps=[r2] + war.w(kB2))
                war.r(("s1row",), bcr)
                evr = b.step("act", lambda a, tc=tc: a.activation(
                    rstdb[tc][:], pool8[7][:], AF.Identity),
                    deps=[bcr] + war.w(("rstdb", tc)))
                war.r(kB2, evr)
                evrs.append(evr)
            wrl = b.dma("sp", lambda s, gi=gi: s.dma_start(
                wrow10[0:8, :], g8_dram[0:8, gi * 1024:(gi + 1) * 1024]),
                deps=[g8w] + war.w(("wrow10",)))
            return dict(uld=uld, evrs=evrs, wrl=wrl)

        pstate = prep(0, prep_load(0))
        out_jobs = []

        def flush_out_jobs():
            while out_jobs:
                out_jobs.pop(0)()

        for gi in range(ngrp):
            up = 0
            uld, evrs, wrl = pstate["uld"], pstate["evrs"], pstate["wrl"]
            xn_done = []
            for tc in range(2):
                for dc in range(NCH):
                    xs = b.step("dve", lambda v, dc=dc, tc=tc, up=up: v.tensor_tensor(
                        xn_of(dc)[:, tc * 512:(tc + 1) * 512],
                        u_of(up, dc)[:, tc * 512:(tc + 1) * 512], rstdb[tc][:],
                        ALU.mult),
                        deps=[evrs[tc], uld] + war.w(("xn", dc, tc)))
                    war.r(("u", up), xs)
                    war.r(("rstdb", tc), xs)
                    xn_done.append(xs)
            for ei, e in enumerate(EXPERTS):
                routed = e < 8
                if routed:
                    grl = b.dma("sp", lambda s, e=e, gi=gi: s.dma_start(
                        grow[e % 2][:], g8_dram[e:e + 1, gi * 1024:(gi + 1) * 1024]),
                        deps=[pstate["wrl"]] + war.w(("grow",)))
                    gsteps = []
                    for half in range(2):
                        bcg = b.step("pe", lambda t, e=e, half=half: t.matmul(
                            pool8[7][:], C["ones_row_b"][:],
                            grow[e % 2][0:1, half * 512:(half + 1) * 512],
                            start=True, stop=True),
                            deps=[grl] + war.w(kB2))
                        evg = b.step("act", lambda a, half=half: a.activation(
                            gbt[:, half * 512:(half + 1) * 512], pool8[7][:],
                            AF.Identity),
                            deps=[bcg] + (war.w(("gbt",)) if half == 0 else []))
                        war.r(kB2, evg)
                        gsteps.append(evg)
                    war.r(("grow",), gsteps[-1])
                wh = [wb[:, 0:8192], wb[:, 8192:16384]]
                w1t_r = lambda rg, dc, hcin: wh[rg][:, dc * 1024 + hcin * 128:
                                                    dc * 1024 + hcin * 128 + 128]
                w2t_r = lambda rg, hcq, c: wh[rg][:, (hcq // 4) * 4096 +
                                                  (hcq % 4) * 1024 + c * 128:
                                                  (hcq // 4) * 4096 +
                                                  (hcq % 4) * 1024 + c * 128 + 128]
                hdeps = xn_done if ei == 0 else []
                last_h = None
                w1l = {}

                def load_w1q(q, e=e):
                    rg = q % 2
                    w1l[q] = b.dma("pool", lambda g, e=e, q=q, rg=rg: [
                        (g.dma_start(wh[rg][:, dc * 1024:(dc + 1) * 1024],
                                     w1_scr[e, dc * 128:(dc + 1) * 128,
                                            q * 1024:(q + 1) * 1024]), 16)
                        for dc in range(NCH)], deps=war.w(("wh", rg)), n=NCH)
                load_w1q(0)
                load_w1q(1)
                for q in range(4):
                    rg = q % 2
                    for tc in range(2):
                        for hcin in range(8):
                            hc = q * 8 + hcin
                            r = hc % 2
                            mmh = b.step("pe", lambda t, tc=tc, hcin=hcin, rg=rg, r=r: [
                                (t.matmul(pool8[r][:], w1t_r(rg, dc, hcin),
                                          xn_of(dc)[:, tc * 512:(tc + 1) * 512],
                                          start=(dc == 0), stop=(dc == NCH - 1)),
                                 1 if dc == NCH - 1 else 0)
                                for dc in range(NCH)],
                                deps=[w1l[q]] + hdeps + war.w(bk(pool8[r])))
                            war.r(("wh", rg), mmh)
                            for dc in range(NCH):
                                war.r(("xn", dc, tc), mmh)
                            gel = b.step("act", lambda a, e=e, hc=hc, tc=tc, r=r:
                                         a.activation(
                                h_of(hc)[:, tc * 512:(tc + 1) * 512], pool8[r][:],
                                AF.Gelu, bias=b1sb[:, e * 32 + hc:e * 32 + hc + 1]),
                                deps=[mmh]
                                + (war.w(("h", hc)) if tc == 0 else [])
                                + (war.w(("uslice",)) if tc == 0 and hc >= 28 else [])
                                + (war.w(("outst",)) if tc == 0 and hc in (24, 25)
                                   else []))
                            war.r(bk(pool8[r]), gel)
                            last_h = gel
                    if q + 2 < 4:
                        load_w1q(q + 2)
                    if ei == 0 and q == 1:
                        flush_out_jobs()
                for tc in range(2):
                    mmw = {}
                    for q in range(4):
                        rg = q % 2
                        w2l = b.dma("pool", lambda g, e=e, q=q, rg=rg: [
                            (g.dma_start(
                                wh[rg][:, bq2 * 4096:(bq2 + 1) * 4096].rearrange(
                                    "p (s c) -> p s c", c=1024),
                                w2_scr[e, q * 1024 + bq2 * 512:
                                       q * 1024 + (bq2 + 1) * 512, :].rearrange(
                                    "(s p) c -> p s c", p=128)), 16)
                            for bq2 in range(2)],
                            deps=war.w(("wh", rg)) + ([last_h] if q == 0 else []),
                            n=2)
                        for c in range(NCH):
                            def w2fn(t, e=e, ei=ei, c=c, tc=tc, q=q, rg=rg):
                                out = []
                                if ei == 0 and q == 0:
                                    out.append((t.matmul(
                                        pool8[c][:], b2sb[0:NE, c * 128:(c + 1) * 128],
                                        wrow10[0:NE, tc * 512:(tc + 1) * 512],
                                        start=True, stop=False,
                                        skip_group_check=True), 0))
                                for hcq in range(8):
                                    out.append((t.matmul(
                                        pool8[c][:], w2t_r(rg, hcq, c),
                                        h_of(q * 8 + hcq)[:, tc * 512:(tc + 1) * 512],
                                        start=(hcq == 0 and q == 0 and ei != 0),
                                        stop=(hcq == 7 and q == 3),
                                        skip_group_check=True),
                                        1 if hcq == 7 else 0))
                                return out
                            mmw[(c, q)] = b.step(
                                "pe", w2fn,
                                deps=[w2l, last_h, wrl]
                                + (war.w(bk(pool8[c])) if q == 0 else [])
                                + [mmw.get((c, q - 1))])
                            war.r(("wh", rg), mmw[(c, q)])
                            war.r(("wrow10",), mmw[(c, q)])
                            for hcq in range(8):
                                war.r(("h", q * 8 + hcq), mmw[(c, q)])
                    for c in range(NCH):
                        aslice = acc[:, c * 1024 + tc * 512: c * 1024 + tc * 512 + 512]
                        fin = mmw[(c, 3)]
                        if ei == 0:
                            sp_ = b.step("dve", lambda v, aslice=aslice, c=c:
                                         v.tensor_copy(aslice, pool8[c][:]),
                                         deps=[fin] + war.w(("acc", c, tc)))
                        elif not routed:
                            sp_ = b.step("dve", lambda v, aslice=aslice, c=c:
                                         v.tensor_tensor(aslice, aslice, pool8[c][:],
                                                         ALU.add), deps=[fin])
                        else:
                            g1 = b.step("dve", lambda v, c=c, tc=tc: v.tensor_tensor(
                                spt[:], pool8[c][:], gbt[:, tc * 512:(tc + 1) * 512],
                                ALU.mult), deps=[fin, gsteps[-1]] + war.w(("spt",)))
                            war.r(("gbt",), g1)
                            sp_ = b.step("dve", lambda v, aslice=aslice: v.tensor_tensor(
                                aslice, aslice, spt[:], ALU.add), deps=[g1])
                            war.r(("spt",), sp_)
                        war.r(bk(pool8[c]), sp_)
                        war.r(("accdone", c, tc), sp_)
                if ei == 0:
                    if gi + 1 < ngrp:
                        next_uld = prep_load(gi + 1)
                if ei == 2 and gi + 1 < ngrp:
                    pstate = prep(gi + 1, next_uld)
            rbuf0 = ln["rbuf"]
            for tc in range(2):
                rb = rbuf0 if tc == 0 else rbuf1
                if tc == 1 and os.environ.get("NO_ALIAS"):
                    rb = rbuf0
                    flush_out_jobs()
                url = b.dma("sp", lambda s, gi=gi, tc=tc: [
                    (s.dma_start(uslice[:, dc * 512:(dc + 1) * 512],
                                 xfm_in[dc, :, gi * 1024 + tc * 512:
                                        gi * 1024 + tc * 512 + 512]), 16)
                    for dc in range(NCH)],
                    deps=(sum((war.w(("h", hx)) for hx in range(28, 32)), [])
                          if tc == 0 else [])
                    + war.w(("uslice",)), n=NCH)
                rcs = None
                for c in range(NCH):
                    rcs = b.step("dve", lambda v, c=c, tc=tc, rb=rb:
                                 v.scalar_tensor_tensor(
                        rb[:, c * 512:(c + 1) * 512],
                        uslice[:, c * 512:(c + 1) * 512], 2.0,
                        acc[:, c * 1024 + tc * 512: c * 1024 + tc * 512 + 512],
                        ALU.mult, ALU.add),
                        deps=[url] + war.w(("rb", tc, c)) + war.w(("accdone", c, tc))
                        + (war.w(("h", 16 + c)) if tc == 1 else []))
                    war.r(("uslice",), rcs)
                    war.r(("acc", c, tc), rcs)
                ln["rbuf"] = rb
                ln["_kr"] = tc
                ln["_outs"] = []
                out_step = emit_ln(p, b, ln, war, rcs, C["eps_ln"], None, [],
                                   use_rbb=False, out_inplace_f32=True)
                ln["rbuf"] = rbuf0
                ln["_kr"] = 0
                outs_per_c = list(ln["_outs"])

                def emit_out(gi=gi, tc=tc, outs_per_c=outs_per_c, rb=rb):
                    nonlocal st_last
                    for blk in range(4):
                        tglob = gi * 1024 + tc * 512 + blk * 128
                        bi, s0 = tglob // seq, tglob % seq
                        evs = []
                        for c in range(NCH):
                            j = blk * NCH + c
                            r = 2 + j % 4
                            trp = b.step("pe", lambda t, c=c, blk=blk, r=r, rb=rb:
                                         t.transpose(
                                pool8[r][:, 0:128],
                                rb[:, c * 512 + blk * 128: c * 512 + blk * 128 + 128],
                                C["ident_f"][:]),
                                deps=[outs_per_c[c]] + war.w(bk(pool8[r])))
                            war.r(("rb", tc, c), trp)
                            if tc == 1:
                                war.r(("h", 16 + c), trp)
                            ev2 = b.step("dve", lambda v, c=c, blk=blk, r=r:
                                         v.tensor_copy(
                                outst[blk % 2][:, c * 128:(c + 1) * 128],
                                pool8[r][:, 0:128]),
                                deps=[trp] + (war.w(("outst",)) if c == 0 else []))
                            war.r(bk(pool8[r]), ev2)
                            evs.append(ev2)
                        st = b.dma("sp", lambda s, bi=bi, s0=s0, blk=blk: s.dma_start(
                            out_ap[bi, s0:s0 + 128, :], outst[blk % 2][:]),
                            deps=[evs[-1]])
                        war.r(("outst",), st)
                        st_last = st
                out_jobs.append(emit_out)
        flush_out_jobs()
        b.step("sp", lambda s: s.nop(), deps=[st_last])
        b.run()


NB_PER_CORE = 4
SEQ = 2048
N_CORES = 8

WKEYS = ["L0_attn_Wq", "L0_attn_bq", "L0_attn_Wk", "L0_attn_bk", "L0_attn_Wv",
         "L0_attn_bv", "L0_attn_Wo", "L0_attn_bo", "L0_ffn_Wr", "L0_ffn_Wi",
         "L0_ffn_Wo", "L1_attn_Wq", "L1_attn_Wk", "L1_attn_Wv", "L1_attn_Wo",
         "L1_ffn_g_shared", "L1_ffn_W1_s", "L1_ffn_b1_s", "L1_ffn_W2_s",
         "L1_ffn_b2_s", "L1_ffn_g_routed", "L1_ffn_W1_r", "L1_ffn_b1_r",
         "L1_ffn_W2_r", "L1_ffn_b2_r", "L1_ffn_centroids"]
WSHAPES = {
    "L0_attn_Wq": [1024, 1024], "L0_attn_bq": [1024],
    "L0_attn_Wk": [1024, 1024], "L0_attn_bk": [1024],
    "L0_attn_Wv": [1024, 1024], "L0_attn_bv": [1024],
    "L0_attn_Wo": [1024, 1024], "L0_attn_bo": [1024],
    "L0_ffn_Wr": [1024, 8], "L0_ffn_Wi": [8, 1024, 1024], "L0_ffn_Wo": [8, 1024, 1024],
    "L1_attn_Wq": [1024, 1024], "L1_attn_Wk": [1024, 1024],
    "L1_attn_Wv": [1024, 1024], "L1_attn_Wo": [1024, 1024],
    "L1_ffn_g_shared": [1024], "L1_ffn_W1_s": [2, 1024, 4096],
    "L1_ffn_b1_s": [2, 4096], "L1_ffn_W2_s": [2, 4096, 1024],
    "L1_ffn_b2_s": [2, 1024], "L1_ffn_g_routed": [1024],
    "L1_ffn_W1_r": [8, 1024, 4096], "L1_ffn_b1_r": [8, 4096],
    "L1_ffn_W2_r": [8, 4096, 1024], "L1_ffn_b2_r": [8, 1024],
    "L1_ffn_centroids": [1024, 8],
}


def build_program(nb=NB_PER_CORE, seq=SEQ):
    nc = bass.Bass("TRN2", target_bir_lowering=False, detect_race_conditions=False)
    T = nb * seq
    x_in = nc.dram_tensor("L0_attn_x", [nb, seq, D], F32, kind="ExternalInput")
    W = {k: nc.dram_tensor(k, WSHAPES[k], F32, kind="ExternalInput").ap()
         for k in WKEYS}
    out = nc.dram_tensor("out", [nb, seq, D], F32, kind="ExternalOutput")
    xfm = [nc.dram_tensor(f"xfm{i}", [NCH, 128, T], BF16, kind="Internal").ap()
           for i in range(4)]
    wi_scr = nc.dram_tensor("wi_scr", [8, 1024, 1024], BF16, kind="Internal").ap()
    wo_scr = nc.dram_tensor("wo_scr", [8, 8, 128, 1024], BF16, kind="Internal").ap()
    w1_scr = nc.dram_tensor("w1_scr", [10, 1024, 4096], BF16, kind="Internal").ap()
    w2_scr = nc.dram_tensor("w2_scr", [10, 4096, 1024], BF16, kind="Internal").ap()

    with contextlib.ExitStack() as stack:
        p = Prog(nc, stack)
        emit_constants(p)
        with nc.sbuf_tensor("g_r", [128, 8], F32) as gr_sb, \
                nc.sbuf_tensor("g_s", [128, 8], F32) as gs_sb:
            b = p.block("gains")
            load_bias_cols(b, gr_sb, W["L1_ffn_g_routed"], 8)
            load_bias_cols(b, gs_sb, W["L1_ffn_g_shared"], 8)
            b.run()
            jobs = []
            for e in range(8):
                jobs.append((W["L0_ffn_Wi"][e], wi_scr[e], "copy", None))
                jobs.append((W["L0_ffn_Wo"][e], wo_scr[e], "tile128", None))
                jobs.append((W["L1_ffn_W1_r"][e], w1_scr[e], "copy", gr_sb))
                jobs.append((W["L1_ffn_W2_r"][e], w2_scr[e], "copy", None))
            for e in range(2):
                jobs.append((W["L1_ffn_W1_s"][e], w1_scr[8 + e], "copy", gs_sb))
                jobs.append((W["L1_ffn_W2_s"][e], w2_scr[8 + e], "copy", None))
            emit_phase_t(p, x_in.ap(), xfm[0], nb, seq)
            emit_l0_attn(p, xfm[0], xfm[1],
                         {k: W["L0_attn_" + k.lower() if k[0] == "b" else "L0_attn_" + k]
                          for k in ["Wq", "bq", "Wk", "bk", "Wv", "bv", "Wo", "bo"]},
                         nb, seq, convert_jobs=jobs)
        emit_l0_ffn(p, xfm[1], xfm[2], wi_scr, wo_scr, W["L0_ffn_Wr"], nb * seq)
        emit_l1_attn(p, xfm[2], xfm[3],
                     {k: W["L1_attn_" + k] for k in ["Wq", "Wk", "Wv", "Wo"]},
                     nb, seq)
        b1_list = [W["L1_ffn_b1_r"][e] for e in range(8)] + \
                  [W["L1_ffn_b1_s"][e] for e in range(2)]
        emit_l1_ffn(p, xfm[3], out.ap(), w1_scr, w2_scr, b1_list,
                    W["L1_ffn_b2_r"], W["L1_ffn_b2_s"], W["L1_ffn_centroids"],
                    nb, seq)
    return nc


_NC_CACHE = {}


def kernel(**inputs):
    key = "full"
    if key not in _NC_CACHE:
        _NC_CACHE[key] = build_program()
    nc = _NC_CACHE[key]
    x = np.ascontiguousarray(inputs["L0_attn_x"])
    shards = np.split(x, N_CORES, axis=0)
    in_maps = []
    for c in range(N_CORES):
        m = {"L0_attn_x": np.ascontiguousarray(shards[c])}
        for k in WKEYS:
            m[k] = np.ascontiguousarray(inputs[k])
        in_maps.append(m)
    res = run_bass_kernel_spmd(nc, in_maps, core_ids=list(range(N_CORES)))
    return np.concatenate([res.results[c]["out"] for c in range(N_CORES)], axis=0)
